# version 03
# baseline (speedup 1.0000x reference)
from contextlib import ExitStack

import numpy as np
import ml_dtypes
import concourse.bass as bass
import concourse.mybir as mybir
from concourse.bass_utils import run_bass_kernel_spmd
from concourse.aot_env import allow_ici_all_to_all

F32 = mybir.dt.float32
BF16 = mybir.dt.bfloat16
I32 = mybir.dt.int32
AF = mybir.ActivationFunctionType
ALU = mybir.AluOpType
AXX = mybir.AxisListType.X

NCORES = 8
B = 2
S = 4096
D = 4096
HC = 4
SCALE = 192.0 ** -0.5
NEGC = -0.5 * SCALE
EPS = 1e-5
NCH = D // 128
TOWN = 1024

DEBUG_PHASE = 0
DEQ_CONTIG_TEST = False
CC_DELAY_OPS = 4000


class Prog:
    def __init__(self, nc, es):
        names = ["dS", "dA", "dG", "cc",
                 "mPE", "mACT", "mDVE", "mPOOL",
                 "sq", "smm", "nrm",
                 "wc", "kv", "kve", "kn", "tp", "tpe",
                 "qev", "bia", "qkt", "exp", "av", "msk",
                 "woc", "woe", "wof", "ff", "ffe", "dq", "ccd", "ab", "ew", "ex", "fw", "fx"]
        self.sem = {}
        self.cnt = {}
        for n in names:
            self.sem[n] = es.enter_context(nc.semaphore("s_" + n))
            self.cnt[n] = 0

    def inc(self, instr, name, n=1):
        instr.then_inc(self.sem[name], n)
        self.cnt[name] += n
        return self.cnt[name]

    def w(self, eng, name, val=None):
        v = self.cnt[name] if val is None else val
        if v > 0:
            eng.wait_ge(self.sem[name], v)


def build_program(debug_phase=0):
    nc = bass.Bass("TRN2", target_bir_lowering=False)

    x_fm = nc.dram_tensor("x_fm", [D, TOWN], F32, kind="ExternalInput").ap()
    wq_s = nc.dram_tensor("wq_s", [D, HC * 128], F32, kind="ExternalInput").ap()
    wk_s = nc.dram_tensor("wk_s", [D, HC * 128], F32, kind="ExternalInput").ap()
    wv_s = nc.dram_tensor("wv_s", [D, HC * 128], F32, kind="ExternalInput").ap()
    wo_in = nc.dram_tensor("wo", [D, D], F32, kind="ExternalInput").ap()
    pw = nc.dram_tensor("pw", [D, 512], I32, kind="ExternalInput").ap()
    qsc = nc.dram_tensor("qsc", [D, 32], F32, kind="ExternalInput").ap()
    fbias = nc.dram_tensor("fbias", [D], F32, kind="ExternalInput").ap()
    ident = nc.dram_tensor("ident", [128, 128], BF16, kind="ExternalInput").ap()

    out_fm = nc.dram_tensor("out_fm", [D, TOWN], F32, kind="ExternalOutput").ap()
    dbg = None
    if debug_phase:
        dbg = nc.dram_tensor("dbg", [128, 8192], F32, kind="ExternalOutput").ap()

    ln_self = nc.dram_tensor("ln_self", [D, TOWN], BF16)
    ln_all = nc.dram_tensor("ln_all", [NCORES * D, TOWN], BF16)
    nc.dram_tensor("pad_lnall", [512, 1024], F32)
    attn_a2a = nc.dram_tensor("attn_a2a", [NCORES * HC * 128, TOWN], BF16)
    attn_own = nc.dram_tensor("attn_own", [NCORES * HC * 128, TOWN], BF16)
    nc.dram_tensor("pad_own", [512, 1024], F32)
    x1_scr = nc.dram_tensor("x1_scr", [D, TOWN], F32)
    w_scr = nc.dram_tensor("w_scr", [D, D], BF16)
    wt_scr = nc.dram_tensor("wt_scr", [D, D], BF16)

    es = ExitStack()
    block = es.enter_context(nc.Block())
    P = Prog(nc, es)
    sbt = lambda *a: es.enter_context(nc.sbuf_tensor(*a))

    ones_col_f = sbt("ones_col_f", [128, 1], F32)
    ones_col_b = sbt("ones_col_b", [128, 1], BF16)
    ones_row_f = sbt("ones_row_f", [1, 128], F32)
    negc_row = sbt("negc_row", [1, 128], F32)
    ident_b = sbt("ident_b", [128, 128], BF16)
    ones_tile = sbt("ones_tile", [128, 512], BF16)
    mask4 = sbt("mask4", [128, 4 * 512], BF16)
    scale_sb = sbt("scale_sb", [128, NCH * 32], F32)
    bias_sb = sbt("bias_sb", [128, NCH], F32)
    kmax_sb = sbt("kmax_sb", [1, HC], F32)
    srow = sbt("srow", [1, 16], F32)
    bias_hh = sbt("bias_hh", [128, 1], F32)
    rec_row = sbt("rec_row", [1, 512], F32)
    recB = sbt("recB", [128, 512], F32)
    def cc_quiet():
        cc_now = P.cnt["cc"]

        @block.vector
        def _(v, _cc=cc_now):
            P.w(v, "cc", _cc)
            ins = None
            for _ in range(CC_DELAY_OPS):
                ins = v.tensor_copy(ones_tile[:, 0:256], ones_tile[:, 256:512])
            if ins is None:
                ins = v.engine_nop()
            P.inc(ins, "ccd")

    @block.vector
    def _(v):
        v.memset(ones_col_f[:], 1.0)
        v.memset(ones_col_b[:], 1.0)
        v.memset(ones_row_f[:], 1.0)
        v.memset(negc_row[:], NEGC)
        P.inc(v.memset(ones_tile[:], 1.0), "mDVE")

    @block.sync
    def _(sy):
        P.inc(sy.dma_start(out=ident_b[:], in_=ident), "dS", 16)
        for och in range(NCH):
            P.inc(sy.dma_start(out=scale_sb[:, och * 32:(och + 1) * 32],
                               in_=qsc[och * 128:(och + 1) * 128, 0:32]), "dS", 16)
            P.inc(sy.dma_start(out=bias_sb[:, och:och + 1],
                               in_=fbias[och * 128:(och + 1) * 128]), "dS", 16)

    @block.gpsimd
    def _(g):
        P.w(g, "mDVE")
        for r in range(4):
            ins = g.affine_select(mask4[:, r * 512:(r + 1) * 512], ones_tile[:],
                                  pattern=[[1, 512]], compare_op=ALU.is_ge,
                                  fill=0.0, base=-(r * 128), channel_multiplier=-1)
        P.inc(ins, "mPOOL")

    if debug_phase == 6:
        with nc.sbuf_tensor("m6a", [128, 1024], F32) as m6a, \
             nc.sbuf_tensor("m6b", [128, 1024], BF16) as m6b:
            @block.sync
            def _(sy):
                for ch in range(NCH):
                    P.inc(sy.dma_start(out=m6a[:],
                                       in_=x_fm[ch * 128:(ch + 1) * 128, :]), "dS", 16)
                    P.w(sy, "dS")
                    P.w(sy, "nrm")

                    @block.vector
                    def _(v):
                        P.w(v, "dS")
                        P.inc(v.tensor_copy(m6b[:], m6a[:]), "nrm")

                    P.w(sy, "nrm")
                    P.inc(sy.dma_start(
                        out=ln_self.ap()[ch * 128:(ch + 1) * 128, :],
                        in_=m6b[:]), "dS", 16)
                    P.w(sy, "dS")

        @block.gpsimd
        def _(g):
            P.w(g, "dS")
            ins = g.collective_compute(
                "AllGather", ALU.bypass, replica_groups=[list(range(NCORES))],
                ins=[ln_self.ap().opt()], outs=[ln_all.ap().opt()])
            P.inc(ins, "cc")

        emit_dbg(nc, block, P, dbg, dram_specs=[(ln_self.ap(), 0, BF16),
                                                (ln_all.ap(), 0, BF16),
                                                (ln_all.ap(), 4 * D, BF16)],
                 wait="cc")
        emit_finish(nc, block, P, out_fm)
        es.close()
        return nc

    if debug_phase in (1, 5):
        with nc.sbuf_tensor("canary", [128, 1024], BF16) as cnr:
            @block.vector
            def _(v):
                P.inc(v.memset(cnr[:], 3.0), "mDVE")

            @block.sync
            def _(sy):
                P.w(sy, "mDVE")
                for ch in range(NCH):
                    P.inc(sy.dma_start(
                        out=ln_self.ap()[ch * 128:(ch + 1) * 128, :],
                        in_=cnr[:]), "dS", 16)

    with nc.sbuf_tensor("lnbuf", [128, NCH * 1024], BF16) as lnbuf, \
         nc.sbuf_tensor("strA", [128, 2 * 1024], F32) as strA, \
         nc.sbuf_tensor("xsqA", [128, 2 * 1024], F32) as xsqA, \
         nc.sbuf_tensor("t1A", [128, 1024], F32) as t1A, \
         nc.sbuf_tensor("rnA", [128, 2048], F32) as rnA, \
         nc.sbuf_tensor("mrA", [1, 4 * 1024], F32) as mrA, \
         nc.psum_tensor("pstA", [128, 2048], F32) as pstA, \
         nc.psum_tensor("pbcA", [128, 2048], F32) as pbcA:

        emit_ln(nc, block, P, x_src=x_fm, out_bf=lnbuf, stream_buf=strA,
                xsq=xsqA, t1buf=t1A, rnB=rnA, pstat=pstA, pbc=pbcA,
                m_rows=mrA, ones_col_f=ones_col_f, ones_row_f=ones_row_f)

        if debug_phase in (9,):
            emit_dbg(nc, block, P, dbg,
                     sbuf_specs=[(lnbuf, 0), (rnA, 0), (rnA, 1024), (strA, 0)],
                     sbuf_f32=[False, True, True, True], wait_mdve=True)

        with nc.sbuf_tensor("stg", [128, 2048], BF16) as stg:
            for ch in range(NCH):
                mdve_c = P.cnt["mDVE"]
                dS_c = P.cnt["dS"]

                @block.vector
                def _(v, _ch=ch, _d=dS_c):
                    P.w(v, "dS", _d)
                    ins = v.tensor_copy(stg[:, (_ch % 2) * 1024:(_ch % 2) * 1024 + 1024],
                                        lnbuf[:, _ch * 1024:(_ch + 1) * 1024])
                    P.inc(ins, "mDVE")
                mdve_n = P.cnt["mDVE"]

                @block.sync
                def _(sy, _ch=ch, _m=mdve_n):
                    P.w(sy, "mDVE", _m)
                    P.inc(sy.dma_start(
                        out=ln_self.ap()[_ch * 128:(_ch + 1) * 128, :],
                        in_=stg[:, (_ch % 2) * 1024:(_ch % 2) * 1024 + 1024]), "dS", 16)

        if debug_phase == 5:
            with nc.sbuf_tensor("canary5", [128, 1024], BF16) as cnr5:
                @block.vector
                def _(v):
                    P.w(v, "nrm")
                    P.inc(v.memset(cnr5[:], 3.0), "mDVE")

                @block.sync
                def _(sy):
                    P.w(sy, "mDVE")
                    for ch in range(NCH):
                        P.inc(sy.dma_start(
                            out=ln_self.ap()[ch * 128:(ch + 1) * 128, :],
                            in_=cnr5[:]), "dS", 16)

    if debug_phase == 9:
        emit_finish(nc, block, P, out_fm)
        es.close()
        return nc

    with nc.sbuf_tensor("flsh", [128, 1024], BF16) as flsh:
        @block.sync
        def _(sy):
            P.w(sy, "dS")
            P.inc(sy.dma_start(out=flsh[:], in_=ln_self.ap()[0:128, :]), "dS", 16)

    @block.gpsimd
    def _(g):
        P.w(g, "dS")
        P.w(g, "dA")
        P.w(g, "dG")
        ins = g.collective_compute(
            "AllGather", ALU.bypass, replica_groups=[list(range(NCORES))],
            ins=[ln_self.ap().opt()], outs=[ln_all.ap().opt()])
        P.inc(ins, "cc")

    cc_quiet()

    if debug_phase in (1, 5):
        emit_dbg(nc, block, P, dbg, dram_specs=[(ln_self.ap(), 0, BF16),
                                                (ln_self.ap(), 2048, BF16),
                                                (ln_all.ap(), 0, BF16),
                                                (ln_all.ap(), 4 * D, BF16)],
                 wait="cc")
        emit_finish(nc, block, P, out_fm)
        es.close()
        return nc

    dq_bufs = None
    if debug_phase not in (2, 21, 3):
        dq_bufs = (es.enter_context(nc.sbuf_tensor("pkq", [128, 512], I32)),
                   es.enter_context(nc.sbuf_tensor("shq", [128, 512], I32)),
                   es.enter_context(nc.sbuf_tensor("dqq", [128, 4096], BF16)))
        for _c in make_dequant_chunks(nc, block, P, pw, scale_sb,
                                      dq_bufs[0], dq_bufs[1], dq_bufs[2],
                                      w_scr):
            _c()

    with nc.sbuf_tensor("K4", [128, HC * S], BF16) as K4, \
         nc.sbuf_tensor("v4", [128, HC * S], BF16) as v4, \
         nc.sbuf_tensor("lnt", [128, 16 * 1024], BF16) as lnt, \
         nc.sbuf_tensor("qbuf4", [128, HC * 1024], BF16) as qbuf4, \
         nc.sbuf_tensor("qsqb", [128, 1024], BF16) as qsqb, \
         nc.sbuf_tensor("ksqb", [128, 512], BF16) as ksqb, \
         nc.sbuf_tensor("prb0", [128, 1024], BF16) as prb0, \
         nc.sbuf_tensor("prb1", [128, 1024], BF16) as prb1, \
         nc.sbuf_tensor("att0", [128, 512], BF16) as att0, \
         nc.sbuf_tensor("att1", [128, 512], BF16) as att1:

        probt = [prb0, prb1]
        attn_b = [att0, att1]

        for b in range(B):
            with nc.sbuf_tensor(f"wkb{b}", [128, NCH * 512], BF16) as wkb, \
                 nc.sbuf_tensor(f"wvb{b}", [128, NCH * 512], BF16) as wvb, \
                 nc.sbuf_tensor(f"wldB{b}", [128, 1024], F32) as wldB, \
                 nc.psum_tensor(f"PK{b}", [128, 2048], F32) as PK, \
                 nc.psum_tensor(f"PV{b}", [128, 2048], F32) as PV:
                emit_phase_B(nc, block, P, b, ln_all, wk_s, wv_s, K4, v4, lnt,
                             wkb, wvb, wldB, PK, PV, ksqb, kmax_sb, srow,
                             ones_col_b)

            if debug_phase == 21 and b == 0:
                @block.vector
                def _(v):
                    P.w(v, "kve")
                    P.w(v, "kn")
                    P.inc(v.engine_nop(), "mDVE")
                emit_dbg(nc, block, P, dbg,
                         sbuf_specs=[(K4, 0), (K4, 4096), (v4, 0), (K4, 2048)],
                         wait_mdve=True)
            if debug_phase == 21:
                continue

            with nc.sbuf_tensor(f"wqb{b}", [128, NCH * 512], BF16) as wqb, \
                 nc.sbuf_tensor(f"wldC{b}", [128, 512], F32) as wldC:
                with nc.psum_tensor(f"ptp{b}", [128, 2048], BF16) as ptp:
                    emit_phase_Cpre(nc, block, P, b, v4, ident_b, ptp,
                                    wq_s, wqb, wldC)

                if debug_phase == 2 and b == 0:
                    @block.vector
                    def _(v):
                        P.w(v, "tpe")
                        P.w(v, "wc")
                        P.w(v, "kve")
                        P.inc(v.engine_nop(), "mDVE")
                    emit_dbg(nc, block, P, dbg,
                             sbuf_specs=[(K4, 0), (K4, 4096), (v4, 0), (wqb, 0)], wait_mdve=True)

                if debug_phase in (2,):
                    continue

                bg = []
                with nc.psum_tensor(f"psc{b}", [128, 2048], F32) as psc, \
                     nc.psum_tensor(f"psc2{b}", [128, 2048], F32) as psc2:
                    emit_phase_C(nc, block, P, b, ln_all, attn_a2a, K4, v4,
                                 lnt, wqb, qbuf4, qsqb, probt, attn_b, mask4,
                                 bias_hh, kmax_sb, srow, ones_col_b,
                                 ones_row_f, negc_row, rec_row, recB,
                                 psc, psc2, bg=bg)


        if debug_phase not in (2, 21, 3, 4):
            with nc.sbuf_tensor("wttq", [128, 4096], BF16) as wttq:
                emit_w_transpose(nc, block, P, w_scr, wt_scr, wttq)

    if debug_phase in (2, 21):
        emit_finish(nc, block, P, out_fm)
        es.close()
        return nc

    if debug_phase == 3:
        emit_dbg(nc, block, P, dbg,
                 dram_specs=[(attn_a2a.ap(), 0, BF16),
                             (attn_a2a.ap(), 128, BF16),
                             (attn_a2a.ap(), 512, BF16),
                             (attn_a2a.ap(), 2048, BF16)],
                 wait="dA")
        emit_finish(nc, block, P, out_fm)
        es.close()
        return nc

    @block.gpsimd
    def _(g):
        P.w(g, "dS")
        P.w(g, "dA")
        P.w(g, "dG")
        P.w(g, "dq")
        with allow_ici_all_to_all():
            ins = g.collective_compute(
                "AllToAll", ALU.bypass, replica_groups=[list(range(NCORES))],
                ins=[attn_a2a.ap().opt()], outs=[attn_own.ap().opt()])
        P.inc(ins, "cc")

    cc_quiet()

    if debug_phase == 40:
        emit_dbg(nc, block, P, dbg, dram_specs=[(attn_own.ap(), 0, BF16),
                                                (attn_own.ap(), 512, BF16),
                                                (attn_own.ap(), 2048, BF16),
                                                (attn_own.ap(), 3584, BF16)],
                 wait="cc")
        emit_finish(nc, block, P, out_fm)
        es.close()
        return nc

    abuf4dump = None
    x1snap = None
    absnap = None
    if debug_phase == 4:
        x1snap = es.enter_context(nc.sbuf_tensor("x1snap", [128, 1024], F32))
    if debug_phase == 52:
        absnap = es.enter_context(nc.sbuf_tensor("absnap", [128, 2048], BF16))
    with nc.sbuf_tensor("abuf", [128, NCH * 1024], BF16) as abuf, \
         nc.sbuf_tensor("wldE0", [128, 512], F32) as wldE0, \
         nc.sbuf_tensor("wldE1", [128, 512], F32) as wldE1, \
         nc.sbuf_tensor("wcb0", [128, 512], BF16) as wcb0, \
         nc.sbuf_tensor("wcb1", [128, 512], BF16) as wcb1, \
         nc.sbuf_tensor("xch0", [128, 1024], F32) as xch0, \
         nc.sbuf_tensor("xch1", [128, 1024], F32) as xch1, \
         nc.sbuf_tensor("x1c0", [128, 1024], F32) as x1c0, \
         nc.sbuf_tensor("x1c1", [128, 1024], F32) as x1c1, \
         nc.psum_tensor("pwoA", [128, 2048], F32) as pwoA, \
         nc.psum_tensor("pwoB", [128, 2048], F32) as pwoB:
        emit_phase_E(nc, block, P, attn_own, wo_in, x_fm, x1_scr, abuf,
                     [wldE0, wldE1], [wcb0, wcb1], [xch0, xch1],
                     [x1c0, x1c1], pwoA, pwoB, x1snap=x1snap, absnap=absnap,
                     recB=recB)
        abuf4dump = abuf
        if debug_phase == 4:
            emit_dbg(nc, block, P, dbg,
                     dram_specs=[(x1_scr.ap(), 0, F32),
                                 (x1_scr.ap(), 512, F32),
                                 (attn_own.ap(), 0, BF16)],
                 wait="dA", sbuf_extra=[(x1snap, 0), (abuf, 0),
                                        (abuf, 2048), (abuf, 31 * 1024)])

    if debug_phase == 4:
        emit_finish(nc, block, P, out_fm)
        es.close()
        return nc

    if debug_phase == 50:
        @block.sync
        def _(sy):
            P.w(sy, "dS")
            P.w(sy, "dA")
        emit_dbg(nc, block, P, dbg, dram_specs=[(w_scr.ap(), 0, BF16),
                                                (w_scr.ap(), 2048, BF16),
                                                (wt_scr.ap(), 0, BF16),
                                                (wt_scr.ap(), 2048, BF16)],
                 wait="dG")
        emit_finish(nc, block, P, out_fm)
        es.close()
        return nc

    with nc.sbuf_tensor("lnt2", [128, NCH * 1024], BF16) as lnt2, \
         nc.sbuf_tensor("strF", [128, 2 * 1024], F32) as strF, \
         nc.sbuf_tensor("xsqF", [128, 2 * 1024], F32) as xsqF, \
         nc.sbuf_tensor("t1F", [128, 1024], F32) as t1F, \
         nc.sbuf_tensor("rnF", [128, 2048], F32) as rnF, \
         nc.sbuf_tensor("mrF", [1, 4 * 1024], F32) as mrF:

        with nc.psum_tensor("pstF", [128, 2048], F32) as pstF, \
             nc.psum_tensor("pbcF", [128, 2048], F32) as pbcF:
            emit_ln(nc, block, P, x_src=x1_scr.ap(), out_bf=lnt2,
                    stream_buf=strF, xsq=xsqF, t1buf=t1F, rnB=rnF,
                    pstat=pstF, pbc=pbcF, m_rows=mrF,
                    ones_col_f=ones_col_f, ones_row_f=ones_row_f,
                    wait_dA_first=True)

        if debug_phase == 51:
            emit_dbg(nc, block, P, dbg,
                     sbuf_specs=[(lnt2, 0), (lnt2, 16 * 1024), (lnt2, 31 * 1024)],
                     wait_mdve=True)

        with nc.sbuf_tensor("wts0", [128, 512], BF16) as wts0, \
             nc.sbuf_tensor("wts1", [128, 512], BF16) as wts1, \
             nc.sbuf_tensor("x1n0", [128, 1024], F32) as x1n0, \
             nc.sbuf_tensor("x1n1", [128, 1024], F32) as x1n1, \
             nc.sbuf_tensor("outb0", [128, 1024], F32) as outb0, \
             nc.sbuf_tensor("outb1", [128, 1024], F32) as outb1, \
             nc.psum_tensor("pfA", [128, 2048], F32) as pfA, \
             nc.psum_tensor("pfB", [128, 2048], F32) as pfB:
            if debug_phase != 51:
                emit_phase_F(nc, block, P, wt_scr, x1_scr, lnt2, bias_sb,
                             out_fm, [wts0, wts1], [x1n0, x1n1],
                             [outb0, outb1], pfA, pfB)

    if debug_phase == 51:
        emit_finish(nc, block, P, out_fm)
        es.close()
        return nc

    if debug_phase == 52:
        @block.sync
        def _(sy):
            P.w(sy, "dA")
        emit_dbg(nc, block, P, dbg, dram_specs=[(x1_scr.ap(), 0, F32),
                                                (x1_scr.ap(), 512, F32),
                                                (attn_own.ap(), 0, BF16),
                                                (attn_own.ap(), 2048, BF16)],
                 wait="dA", sbuf_extra=[(absnap, 0), (absnap, 1024)])

    @block.sync
    def _(sy):
        P.w(sy, "dS")
        P.w(sy, "dA")
        P.w(sy, "dG")
        P.w(sy, "ew")
        P.w(sy, "ex")
        P.w(sy, "fw")
        P.w(sy, "fx")
        P.w(sy, "ab")

    es.close()
    return nc


def emit_ln(nc, block, P, x_src, out_bf, stream_buf, xsq, t1buf, rnB,
            pstat, pbc, m_rows, ones_col_f, ones_row_f, wait_dA_first=False):
    sq0, smm0, nrm0 = P.cnt["sq"], P.cnt["smm"], P.cnt["nrm"]
    dA_now = P.cnt["dA"]

    @block.sync
    def _(sy):
        if wait_dA_first:
            P.w(sy, "dA", dA_now)
        for ch in range(NCH):
            if ch >= 2:
                P.w(sy, "smm", smm0 + ch - 1)
            P.inc(sy.dma_start(
                out=stream_buf[:, (ch % 2) * 1024:(ch % 2) * 1024 + 1024],
                in_=x_src[ch * 128:(ch + 1) * 128, :]), "dS", 16)
    dload = P.cnt["dS"] - 16 * NCH

    def cfull(ch):
        return stream_buf[:, (ch % 2) * 1024:(ch % 2) * 1024 + 1024]

    @block.scalar
    def _(sc):
        for ch in range(NCH):
            P.w(sc, "dS", dload + 16 * (ch + 1))
            if ch >= 2:
                P.w(sc, "smm", smm0 + ch - 1)
            ins = sc.activation(xsq[:, (ch % 2) * 1024:(ch % 2) * 1024 + 1024],
                                cfull(ch), AF.Square)
            P.inc(ins, "sq")

    @block.tensor
    def _(t):
        for ch in range(NCH):
            P.w(t, "sq", sq0 + ch + 1)
            st, sp = (ch == 0), (ch == NCH - 1)
            c = cfull(ch)
            t.matmul(pstat[0:1, 0:512], ones_col_f[:], c[:, 0:512], start=st, stop=sp)
            t.matmul(pstat[0:1, 512:1024], ones_col_f[:], c[:, 512:1024], start=st, stop=sp)
            t.matmul(pstat[0:1, 1024:1536], ones_col_f[:],
                     xsq[:, (ch % 2) * 1024:(ch % 2) * 1024 + 512], start=st, stop=sp)
            ins = t.matmul(pstat[0:1, 1536:2048], ones_col_f[:],
                           xsq[:, (ch % 2) * 1024 + 512:(ch % 2) * 1024 + 1024],
                           start=st, stop=sp)
            P.inc(ins, "smm")

    m = m_rows[0:1, 0:1024]
    var = m_rows[0:1, 1024:2048]
    rstd = m_rows[0:1, 2048:3072]
    negm = m_rows[0:1, 3072:4096]

    @block.vector
    def _(v):
        P.w(v, "smm", smm0 + NCH)
        v.tensor_scalar(m, pstat[0:1, 0:1024], 1.0 / D, None, ALU.mult)
        v.tensor_scalar(var, pstat[0:1, 1024:2048], 1.0 / D, None, ALU.mult)
        v.tensor_tensor(rstd, m, m, ALU.mult)
        v.tensor_sub(var, var, rstd)
        ins = v.tensor_scalar_add(var, var, EPS)
        P.inc(ins, "mDVE")

    @block.scalar
    def _(sc):
        P.w(sc, "mDVE")
        P.inc(sc.activation(rstd, var, AF.Sqrt), "mACT")

    @block.vector
    def _(v):
        P.w(v, "mACT")
        v.reciprocal(var, rstd)
        v.tensor_copy(rstd, var)
        ins = v.scalar_tensor_tensor(negm, m, -1.0, rstd, ALU.mult, ALU.mult)
        P.inc(ins, "mDVE")

    @block.tensor
    def _(t):
        P.w(t, "mDVE")
        t.matmul(pbc[:, 0:512], ones_row_f[:], rstd[0:1, 0:512], start=True, stop=True)
        t.matmul(pbc[:, 512:1024], ones_row_f[:], rstd[0:1, 512:1024], start=True, stop=True)
        t.matmul(pbc[:, 1024:1536], ones_row_f[:], negm[0:1, 0:512], start=True, stop=True)
        ins = t.matmul(pbc[:, 1536:2048], ones_row_f[:], negm[0:1, 512:1024],
                       start=True, stop=True)
        P.inc(ins, "mPE")

    @block.vector
    def _(v):
        P.w(v, "mPE")
        v.tensor_copy(rnB[:, 0:1024], pbc[:, 0:1024])
        ins = v.tensor_copy(rnB[:, 1024:2048], pbc[:, 1024:2048])
        P.inc(ins, "mDVE")
    rn_done = P.cnt["mDVE"]

    @block.sync
    def _(sy):
        P.w(sy, "mDVE", rn_done)
        for ch in range(NCH):
            if ch >= 2:
                P.w(sy, "nrm", nrm0 + ch - 1)
            P.inc(sy.dma_start(
                out=stream_buf[:, (ch % 2) * 1024:(ch % 2) * 1024 + 1024],
                in_=x_src[ch * 128:(ch + 1) * 128, :]), "dS", 16)
    d2 = P.cnt["dS"] - 16 * NCH

    @block.vector
    def _(v):
        for ch in range(NCH):
            P.w(v, "dS", d2 + 16 * (ch + 1))
            v.tensor_tensor(t1buf[:], cfull(ch), rnB[:, 0:1024], ALU.mult)
            ins = v.tensor_tensor(out_bf[:, ch * 1024:(ch + 1) * 1024],
                                  t1buf[:], rnB[:, 1024:2048], ALU.add)
            P.inc(ins, "nrm")
        P.inc(v.engine_nop(), "mDVE")


def emit_phase_B(nc, block, P, b, ln_all, wk_s, wv_s, K4, v4, lnt,
                 wkb, wvb, wldB, PK, PV, ksqb, kmax_sb, srow, ones_col_b):
    wc0 = P.cnt["wc"]

    @block.sync
    def _(sy):
        for ch in range(NCH):
            if ch >= 1:
                P.w(sy, "wc", wc0 + ch)
            P.inc(sy.dma_start(out=wldB[:, 0:512],
                               in_=wk_s[ch * 128:(ch + 1) * 128, :]), "dS", 16)
            P.inc(sy.dma_start(out=wldB[:, 512:1024],
                               in_=wv_s[ch * 128:(ch + 1) * 128, :]), "dS", 16)
    wdl = P.cnt["dS"] - 32 * NCH

    @block.vector
    def _(v):
        for ch in range(NCH):
            P.w(v, "dS", wdl + 32 * (ch + 1))
            v.tensor_copy(wkb[:, ch * 512:(ch + 1) * 512], wldB[:, 0:512])
            ins = v.tensor_copy(wvb[:, ch * 512:(ch + 1) * 512], wldB[:, 512:1024])
            P.inc(ins, "wc")
    wc_done = P.cnt["wc"]

    for tb in range(4):
        rank = 4 * b + tb
        for hp in range(2):
            kve_before = P.cnt["kve"]
            for h2 in range(2):
                kv_prev = P.cnt["kv"]
                qev_prev = P.cnt["qev"]
                cc_now = P.cnt["cc"]

                @block.sync
                def _(sy, _rank=rank, _h2=h2, _kv=kv_prev, _qev=qev_prev,
                      _cc=cc_now):
                    P.w(sy, "cc", _cc)
                    P.w(sy, "ccd", min(_cc, P.cnt["ccd"]))
                    P.w(sy, "kv", _kv)
                    P.w(sy, "qev", _qev)
                    r0 = _rank * D + _h2 * 16 * 128
                    P.inc(sy.dma_start(
                        out=lnt[:].rearrange("p (c t) -> p c t", t=1024),
                        in_=ln_all.ap()[r0:r0 + 16 * 128, :]
                        .rearrange("(c p) t -> p c t", p=128)), "dS", 16)
                lnt_d = P.cnt["dS"]

                @block.tensor
                def _(t, _hp=hp, _h2=h2, _lnt_d=lnt_d, _kve=kve_before):
                    P.w(t, "dS", _lnt_d)
                    P.w(t, "wc", wc_done)
                    if _h2 == 0:
                        P.w(t, "kve", _kve)
                    ins = None
                    for cc16 in range(16):
                        ch = _h2 * 16 + cc16
                        st, sp = (ch == 0), (ch == NCH - 1)
                        for hh in range(2):
                            h = 2 * _hp + hh
                            wkt = wkb[:, ch * 512 + h * 128:
                                      ch * 512 + h * 128 + 128]
                            wvt = wvb[:, ch * 512 + h * 128:
                                      ch * 512 + h * 128 + 128]
                            for half in range(2):
                                rhs = lnt[:, cc16 * 1024 + half * 512:
                                          cc16 * 1024 + half * 512 + 512]
                                sl = slice((hh * 2 + half) * 512,
                                           (hh * 2 + half + 1) * 512)
                                t.matmul(PK[:, sl], wkt, rhs, start=st, stop=sp)
                                ins = t.matmul(PV[:, sl], wvt, rhs,
                                               start=st, stop=sp)
                    P.inc(ins, "kv")

            @block.scalar
            def _(sc, _hp=hp, _tb=tb):
                P.w(sc, "kv")
                ins = None
                for hh in range(2):
                    h = 2 * _hp + hh
                    for half in range(2):
                        off = h * S + _tb * 1024 + half * 512
                        sl = slice((hh * 2 + half) * 512, (hh * 2 + half + 1) * 512)
                        sc.activation(K4[:, off:off + 512], PK[:, sl], AF.Copy)
                        ins = sc.activation(v4[:, off:off + 512], PV[:, sl], AF.Copy)
                P.inc(ins, "kve")

    kn0 = P.cnt["kn"]
    kve_all = P.cnt["kve"]
    mdve0 = P.cnt["mDVE"]
    sqk0 = P.cnt["sq"]

    @block.scalar
    def _(sc):
        P.w(sc, "kve", kve_all)
        for i in range(HC * 8):
            if i >= 1:
                P.w(sc, "kn", kn0 + i)
            h, j = divmod(i, 8)
            ins = sc.activation(ksqb[:], K4[:, h * S + j * 512:h * S + (j + 1) * 512],
                                AF.Square)
            P.inc(ins, "sq")

    @block.tensor
    def _(t):
        for i in range(HC * 8):
            P.w(t, "sq", sqk0 + i + 1)
            if i >= 1:
                P.w(t, "mDVE", mdve0 + i)
            ins = t.matmul(PK[0:1, 0:512], ones_col_b[:], ksqb[:],
                           start=True, stop=True)
            P.inc(ins, "kn")

    @block.vector
    def _(v):
        for i in range(HC * 8):
            h, j = divmod(i, 8)
            P.w(v, "kn", kn0 + i + 1)
            if j == 0:
                ins = v.tensor_reduce(kmax_sb[0:1, h:h + 1], PK[0:1, 0:512],
                                      AXX, ALU.max)
            else:
                v.tensor_reduce(srow[0:1, 0:1], PK[0:1, 0:512], AXX, ALU.max)
                ins = v.tensor_tensor(kmax_sb[0:1, h:h + 1],
                                      kmax_sb[0:1, h:h + 1],
                                      srow[0:1, 0:1], ALU.max)
            P.inc(ins, "mDVE")


def emit_phase_Cpre(nc, block, P, b, v4, ident_b, ptp, wq_s, wqb, wldC):
    tp0, tpe0 = P.cnt["tp"], P.cnt["tpe"]
    kve_all = P.cnt["kve"]
    mdve_all = P.cnt["mDVE"]

    @block.tensor
    def _(t):
        P.w(t, "kve", kve_all)
        P.w(t, "mDVE", mdve_all)
        for i in range(HC * 32):
            h, kt = divmod(i, 32)
            if i >= 2:
                P.w(t, "tpe", tpe0 + i - 1)
            off = h * S + kt * 128
            ins = t.transpose(ptp[:, (i % 2) * 1024:(i % 2) * 1024 + 128],
                              v4[:, off:off + 128], ident_b[:])
            P.inc(ins, "tp")

    @block.scalar
    def _(sc):
        for i in range(HC * 32):
            h, kt = divmod(i, 32)
            P.w(sc, "tp", tp0 + i + 1)
            off = h * S + kt * 128
            ins = sc.activation(v4[:, off:off + 128],
                                ptp[:, (i % 2) * 1024:(i % 2) * 1024 + 128], AF.Copy)
            P.inc(ins, "tpe")

    wc0 = P.cnt["wc"]

    @block.sync
    def _(sy):
        for ch in range(NCH):
            if ch >= 1:
                P.w(sy, "wc", wc0 + ch)
            P.inc(sy.dma_start(out=wldC[:],
                               in_=wq_s[ch * 128:(ch + 1) * 128, :]), "dS", 16)
    wql = P.cnt["dS"] - 16 * NCH

    @block.vector
    def _(v):
        for ch in range(NCH):
            P.w(v, "dS", wql + 16 * (ch + 1))
            ins = v.tensor_copy(wqb[:, ch * 512:(ch + 1) * 512], wldC[:])
            P.inc(ins, "wc")


def emit_phase_C(nc, block, P, b, ln_all, attn_a2a, K4, v4, lnt, wqb,
                 qbuf4, qsqb, probt, attn_b, mask4, bias_hh, kmax_sb, srow,
                 ones_col_b, ones_row_f, negc_row, rec_row, recB,
                 psc, psc2, bg=()):
    bg = list(bg)
    """Attention for 4 heads of batch b over all queries (causal).
    PSUM roles: qproj -> pq4[h] = (psc|psc2)[:, (h%2)*1024:+1024];
    attention -> psc = 4-deep score ring [128,512] slots,
    pav = psc2[:, 0:512], ps_s = psc2[:, 512:1024],
    bias bcast = psc[:, 1024:1025], recB bcast = psc[:, 0:512]."""
    wq_done = P.cnt["wc"]
    tpe_done = P.cnt["tpe"]
    pav = psc2[:, 0:512]
    ps_s = psc2[0:1, 512:1024]
    pq4 = [psc[:, 0:1024], psc[:, 1024:2048],
           psc2[:, 0:1024], psc2[:, 1024:2048]]

    for qb in range(4):
        rank = 4 * b + qb
        for h2 in range(2):
            kv_prev = P.cnt["kv"]
            cc_now = P.cnt["cc"]

            @block.sync
            def _(sy, _rank=rank, _h2=h2, _kv=kv_prev, _cc=cc_now):
                P.w(sy, "cc", _cc)
                P.w(sy, "ccd", min(_cc, P.cnt["ccd"]))
                P.w(sy, "kv", _kv)
                r0 = _rank * D + _h2 * 16 * 128
                P.inc(sy.dma_start(
                    out=lnt[:].rearrange("p (c t) -> p c t", t=1024),
                    in_=ln_all.ap()[r0:r0 + 16 * 128, :]
                    .rearrange("(c p) t -> p c t", p=128)), "dS", 16)
            lnt_d = P.cnt["dS"]

            exp_all = P.cnt["exp"]
            mdve_all = P.cnt["mDVE"]
            bia_all = P.cnt["bia"]
            qev_all = P.cnt["qev"]

            @block.tensor
            def _(t, _h2=h2, _lnt_d=lnt_d, _exp=exp_all, _mdve=mdve_all,
                  _bia=bia_all, _qev=qev_all):
                P.w(t, "dS", _lnt_d)
                P.w(t, "wc", wq_done)
                P.w(t, "tpe", tpe_done)
                if _h2 == 0:
                    P.w(t, "exp", _exp)
                    P.w(t, "mDVE", _mdve)
                    P.w(t, "bia", _bia)
                    P.w(t, "qev", _qev)
                ins = None
                for cc16 in range(16):
                    ch = _h2 * 16 + cc16
                    st, sp = (ch == 0), (ch == NCH - 1)
                    for h in range(HC):
                        wqt = wqb[:, ch * 512 + h * 128: ch * 512 + h * 128 + 128]
                        t.matmul(pq4[h][:, 0:512], wqt,
                                 lnt[:, cc16 * 1024:cc16 * 1024 + 512],
                                 start=st, stop=sp)
                        ins = t.matmul(pq4[h][:, 512:1024], wqt,
                                       lnt[:, cc16 * 1024 + 512:cc16 * 1024 + 1024],
                                       start=st, stop=sp)
                P.inc(ins, "kv")
        kv_qp = P.cnt["kv"]

        @block.scalar
        def _(sc, _kv=kv_qp):
            P.w(sc, "kv", _kv)
            ins = None
            for h in range(HC):
                ins = sc.activation(qbuf4[:, h * 1024:(h + 1) * 1024],
                                    pq4[h], AF.Copy)
            P.inc(ins, "qev")
        qev_done = P.cnt["qev"]

        for h in range(HC):
            mpe_cur = P.cnt["mPE"]

            @block.scalar
            def _(sc, _h=h, _qev=qev_done, _mpe=mpe_cur):
                P.w(sc, "qev", _qev)
                P.w(sc, "mPE", _mpe)
                ins = sc.activation(qsqb[:], qbuf4[:, _h * 1024:(_h + 1) * 1024],
                                    AF.Square)
                P.inc(ins, "sq")
            sq_now = P.cnt["sq"]

            exp_all2 = P.cnt["exp"]
            mdve_all2 = P.cnt["mDVE"]

            @block.tensor
            def _(t, _sq=sq_now, _exp=exp_all2, _mdve=mdve_all2):
                P.w(t, "sq", _sq)
                P.w(t, "exp", _exp)
                P.w(t, "mDVE", _mdve)
                t.matmul(psc[0:1, 0:512], ones_col_b[:], qsqb[:, 0:512],
                         start=True, stop=True)
                ins = t.matmul(psc[0:1, 512:1024], ones_col_b[:],
                               qsqb[:, 512:1024], start=True, stop=True)
                P.inc(ins, "mPE")
            mpe2 = P.cnt["mPE"]

            @block.vector
            def _(v, _h=h, _mpe=mpe2):
                P.w(v, "mPE", _mpe)
                v.tensor_reduce(srow[0:1, 0:1], psc[0:1, 0:512], AXX, ALU.max)
                v.tensor_reduce(srow[0:1, 1:2], psc[0:1, 512:1024], AXX, ALU.max)
                v.tensor_tensor(srow[0:1, 0:1], srow[0:1, 0:1], srow[0:1, 1:2],
                                ALU.max)
                ins = v.tensor_tensor(srow[0:1, 0:1], srow[0:1, 0:1],
                                      kmax_sb[0:1, _h:_h + 1], ALU.add)
                P.inc(ins, "mDVE")
            mdve2 = P.cnt["mDVE"]

            @block.tensor
            def _(t, _mdve=mdve2):
                P.w(t, "mDVE", _mdve)
                ins = t.matmul(psc[:, 1024:1025], negc_row[:], srow[0:1, 0:1],
                               start=True, stop=True)
                P.inc(ins, "mPE")
            mpe3 = P.cnt["mPE"]

            exp_prev = P.cnt["exp"]

            @block.vector
            def _(v, _mpe=mpe3, _exp=exp_prev):
                P.w(v, "mPE", _mpe)
                P.w(v, "exp", _exp)
                ins = v.tensor_copy(bias_hh[:], psc[:, 1024:1025])
                P.inc(ins, "bia")
            bia_now = P.cnt["bia"]

            for qh in range(2):
                n = (qb * 2 + qh + 1) * 4
                npair = n // 2
                qkt0 = P.cnt["qkt"]
                exp0 = P.cnt["exp"]
                av0 = P.cnt["av"]
                msk0 = P.cnt["msk"]
                mdve_fin = P.cnt["mDVE"]
                qs = h * 1024 + qh * 512

                def vtile(i, _h=h):
                    return v4[:, _h * S + i * 128: _h * S + i * 128 + 128]

                def ktile(i, _h=h):
                    return K4[:, _h * S + i * 128: _h * S + i * 128 + 128]

                def consume(t, j, _n=n, _exp0=exp0, _msk0=msk0):
                    P.w(t, "exp", _exp0 + j + 1)
                    ndiag = max(0, 2 * j + 2 - (_n - 4))
                    if ndiag > 0:
                        P.w(t, "msk", _msk0 + ndiag)
                    for tt in (2 * j, 2 * j + 1):
                        sl = probt[j % 2][:, (tt % 2) * 512:(tt % 2) * 512 + 512]
                        st, sp = (tt == 0), (tt == _n - 1)
                        t.matmul(ps_s, ones_col_b[:], sl, start=st, stop=sp)
                        ins = t.matmul(pav, vtile(tt), sl, start=st, stop=sp)
                        P.inc(ins, "av")

                @block.tensor
                def _(t, _n=n, _qs=qs, _exp0=exp0, _mdve=mdve_fin,
                      _bia=bia_now, consume=consume, ktile=ktile):
                    P.w(t, "mDVE", _mdve)
                    P.w(t, "bia", _bia)
                    for i in range(_n):
                        if i >= 4:
                            P.w(t, "exp", _exp0 + (i - 4) // 2 + 1)
                        ins = t.matmul(psc[:, (i % 4) * 512:(i % 4) * 512 + 512],
                                       ktile(i), qbuf4[:, _qs:_qs + 512],
                                       start=True, stop=True)
                        P.inc(ins, "qkt")
                        if i % 2 == 1 and i >= 3:
                            consume(t, (i - 1) // 2 - 1)
                    consume(t, _n // 2 - 1)

                @block.scalar
                def _(sc, _npair=npair, _qkt0=qkt0, _av0=av0, _bia=bia_now):
                    P.w(sc, "bia", _bia)
                    for j in range(_npair):
                        P.w(sc, "qkt", _qkt0 + 2 * j + 2)
                        if j >= 2:
                            P.w(sc, "av", _av0 + 2 * (j - 2) + 2)
                        ins = sc.activation(
                            probt[j % 2][:],
                            psc[:, (j % 2) * 1024:(j % 2) * 1024 + 1024],
                            AF.Exp, bias=bias_hh[:, 0:1], scale=SCALE)
                        P.inc(ins, "exp")

                @block.vector
                def _(v, _n=n, _exp0=exp0):
                    P.w(v, "mPOOL", 1)
                    for d in range(4):
                        i = _n - 4 + d
                        j = i // 2
                        P.w(v, "exp", _exp0 + j + 1)
                        sl = probt[j % 2][:, (i % 2) * 512:(i % 2) * 512 + 512]
                        ins = v.tensor_tensor(sl, sl,
                                              mask4[:, d * 512:(d + 1) * 512],
                                              ALU.mult)
                        P.inc(ins, "msk")

                av_all = P.cnt["av"]
                dA_em = P.cnt["dA"]

                @block.vector
                def _(v, _av=av_all):
                    P.w(v, "av", _av)
                    ins = v.reciprocal(rec_row[:], ps_s)
                    P.inc(ins, "mDVE")
                mdve3 = P.cnt["mDVE"]

                @block.tensor
                def _(t, _mdve=mdve3):
                    P.w(t, "mDVE", _mdve)
                    ins = t.matmul(psc[:, 0:512], ones_row_f[:], rec_row[:],
                                   start=True, stop=True)
                    P.inc(ins, "mPE")
                mpe4 = P.cnt["mPE"]

                gidx = dA_em // 16
                abuf_t = attn_b[gidx % 2]

                @block.vector
                def _(v, _abuf=abuf_t, _dA=dA_em, _mpe=mpe4):
                    P.w(v, "mPE", _mpe)
                    v.tensor_copy(recB[:], psc[:, 0:512])
                    if _dA >= 32:
                        P.w(v, "dA", _dA - 16)
                    ins = v.tensor_tensor(_abuf[:], pav, recB[:], ALU.mult)
                    P.inc(ins, "mDVE")
                mdve4 = P.cnt["mDVE"]

                gcol = b * S + (qb * 2 + qh) * 512
                j2 = gcol // 1024
                coff = gcol % 1024

                @block.scalar
                def _(sc, _abuf=abuf_t, _h=h, _j2=j2, _coff=coff, _mdve=mdve4):
                    P.w(sc, "mDVE", _mdve)
                    P.inc(sc.dma_start(
                        out=attn_a2a.ap()[_j2 * 512 + _h * 128:
                                          _j2 * 512 + (_h + 1) * 128,
                                          _coff:_coff + 512],
                        in_=_abuf[:]), "dA", 16)

            for _ in range(2):
                if bg:
                    bg.pop(0)()


def make_dequant_chunks(nc, block, P, pw, scale_sb, pkq, shq, dqq, w_scr):
    chunks = []

    def chunk(och):
        dq_before = och

        @block.gpsimd
        def _(g, _och=och):
            if _och > 0:
                P.w(g, "dq", _och)
            P.inc(g.dma_start(out=pkq[:],
                              in_=pw[_och * 128:(_och + 1) * 128, :]), "dG", 16)
        dg_load = P.cnt["dG"]

        @block.vector
        def _(v, _och=och, _dg=dg_load):
            P.w(v, "dG", _dg)
            scv = scale_sb[:, _och * 32:(_och + 1) * 32].unsqueeze(2) \
                .broadcast_to([128, 32, 16])
            for e in range(8):
                v.tensor_single_scalar(shq[:], pkq[:], 4 * e,
                                       ALU.logical_shift_left)
                v.tensor_single_scalar(shq[:], shq[:], 28,
                                       ALU.arith_shift_right)
                out_v = dqq[:].rearrange("p (g s e) -> p g s e",
                                         g=32, s=16, e=8)[:, :, :, e]
                in_v = shq[:].rearrange("p (g s) -> p g s", s=16)
                if DEQ_CONTIG_TEST:
                    out_v = dqq[:, e * 512:(e + 1) * 512].rearrange(
                        "p (g s) -> p g s", s=16)
                ins = v.tensor_tensor(out_v, in_v, scv, ALU.mult)
            P.inc(ins, "dq")

        @block.gpsimd
        def _(g, _och=och):
            P.w(g, "dq", _och + 1)
            P.inc(g.dma_start(out=w_scr.ap()[_och * 128:(_och + 1) * 128, :],
                              in_=dqq[:]), "dG", 16)

    for och in range(NCH):
        chunks.append(lambda _o=och: chunk(_o))
    return chunks


def emit_w_transpose(nc, block, P, w_scr, wt_scr, wttq):
    dG_all = P.cnt["dG"]

    @block.sync
    def _(sy):
        P.w(sy, "dG", dG_all)
        for ich in range(NCH):
            P.inc(sy.dma_start(out=wttq[:],
                               in_=w_scr.ap()[:, ich * 128:(ich + 1) * 128],
                               transpose=True), "dS", 16)
            P.w(sy, "dS")
            P.inc(sy.dma_start(out=wt_scr.ap()[ich * 128:(ich + 1) * 128, :],
                               in_=wttq[:]), "dS", 16)
            P.w(sy, "dS")


def emit_phase_E(nc, block, P, attn_own, wo_in, x_fm, x1_scr, abuf,
                 wldE, wcb, xch, x1c, pwoA, pwoB, x1snap=None, absnap=None,
                 recB=None):
    cc_all = P.cnt["cc"]

    @block.sync
    def _(sy):
        P.w(sy, "cc", cc_all)
        P.w(sy, "ccd", P.cnt["ccd"])
        P.w(sy, "dG")
        P.w(sy, "dq")
        P.inc(sy.dma_start(
            out=abuf[:].rearrange("p (f t) -> p f t", t=1024),
            in_=attn_own.ap().rearrange("(f p) t -> p f t", p=128)), "ab", 16)
    ab_d = P.cnt["ab"]

    pw4 = [pwoA[:, 0:1024], pwoA[:, 1024:2048],
           pwoB[:, 0:1024], pwoB[:, 1024:2048]]
    dA_E0 = P.cnt["dA"]

    if absnap is not None:
        @block.vector
        def _(v, _ab=ab_d):
            P.w(v, "ab", _ab)
            v.tensor_copy(absnap[:, 0:1024], abuf[:, 0:1024])
            ins = v.tensor_copy(absnap[:, 1024:2048], abuf[:, 16 * 1024:17 * 1024])
            P.inc(ins, "mDVE")
    else:
        @block.vector
        def _(v, _ab=ab_d):
            P.w(v, "ab", _ab)
            v.tensor_copy(recB[:], abuf[:, 0:512])
            ins = v.tensor_copy(recB[:], abuf[:, 16 * 1024:16 * 1024 + 512])
            P.inc(ins, "mDVE")
    snap_m = P.cnt["mDVE"]

    schedule = [1, 2, 3, 4, 5, 6, 7, 0, 0]
    for it, eg in enumerate(schedule):
        woc0 = P.cnt["woc"]
        woe0 = P.cnt["woe"]

        @block.sync
        def _(sy, _eg=eg, _woc0=woc0):
            for fch in range(NCH):
                if fch >= 2:
                    P.w(sy, "woc", _woc0 + fch - 1)
                P.inc(sy.dma_start(
                    out=wldE[fch % 2][:],
                    in_=wo_in[fch * 128:(fch + 1) * 128,
                              _eg * 512:(_eg + 1) * 512]), "ew", 16)
        wo_d = P.cnt["ew"] - 16 * NCH

        wof0 = P.cnt["wof"]

        @block.vector
        def _(v, _wo_d=wo_d, _it=it):
            for fch in range(NCH):
                g = _it * NCH + fch
                P.w(v, "ew", _wo_d + 16 * (fch + 1))
                if g >= 2:
                    P.w(v, "wof", g - 1)
                ins = v.tensor_copy(wcb[fch % 2][:], wldE[fch % 2][:])
                P.inc(ins, "woc")
        woc_b = P.cnt["woc"] - NCH

        @block.tensor
        def _(t, _woc=woc_b, _ab=ab_d, _woe=woe0, _sm=snap_m):
            P.w(t, "ab", _ab)
            if _sm is not None:
                P.w(t, "mDVE", _sm)
            P.w(t, "woe", _woe)
            ins = None
            for fch in range(NCH):
                P.w(t, "woc", _woc + fch + 1)
                st, sp = (fch == 0), (fch == NCH - 1)
                for e4 in range(4):
                    lh = wcb[fch % 2][:, e4 * 128:(e4 + 1) * 128]
                    t.matmul(pw4[e4][:, 0:512], lh,
                             abuf[:, fch * 1024:fch * 1024 + 512],
                             start=st, stop=sp)
                    ins = t.matmul(pw4[e4][:, 512:1024], lh,
                                   abuf[:, fch * 1024 + 512:fch * 1024 + 1024],
                                   start=st, stop=sp)
                P.inc(ins, "wof")
        wof_e = P.cnt["wof"]

        @block.sync
        def _(sy, _eg=eg, _it=it):
            for e4 in range(4):
                kg = _it * 4 + e4
                k = _eg * 4 + e4
                if kg >= 2:
                    P.w(sy, "woe", kg - 1)
                P.inc(sy.dma_start(
                    out=xch[e4 % 2][:],
                    in_=x_fm[k * 128:(k + 1) * 128, :]), "ex", 16)
        x_d = P.cnt["ex"] - 16 * 4

        @block.vector
        def _(v, _it=it, _mpe=wof_e, _x_d=x_d):
            P.w(v, "wof", _mpe)
            for e4 in range(4):
                kg = _it * 4 + e4
                P.w(v, "ex", _x_d + 16 * (e4 + 1))
                if kg >= 2:
                    P.w(v, "dA", dA_E0 + 16 * (kg - 1))
                ins = v.tensor_add(x1c[e4 % 2][:], pw4[e4], xch[e4 % 2][:])
                P.inc(ins, "woe")
        woe_n = P.cnt["woe"]

        @block.scalar
        def _(sc, _eg=eg, _woe=woe_n):
            for e4 in range(4):
                k = _eg * 4 + e4
                P.w(sc, "woe", _woe - 3 + e4)
                P.inc(sc.dma_start(
                    out=x1_scr.ap()[k * 128:(k + 1) * 128, :],
                    in_=x1c[e4 % 2][:]), "dA", 16)
            if _eg == 0 and x1snap is not None:
                P.w(sc, "dA")
                P.inc(sc.dma_start(out=x1snap[:], in_=x1_scr.ap()[0:128, :]),
                      "dA", 16)


def emit_phase_F(nc, block, P, wt_scr, x1_scr, lnt2, bias_sb, out_fm,
                 wts, x1n, outb, pfA, pfB):
    pf4 = [pfA[:, 0:1024], pfA[:, 1024:2048],
           pfB[:, 0:1024], pfB[:, 1024:2048]]
    nrm_all = P.cnt["nrm"]
    dA_F0 = P.cnt["dA"]

    for og in range(8):
        ff0 = P.cnt["ff"]
        ffe0 = P.cnt["ffe"]

        @block.sync
        def _(sy, _og=og, _ff0=ff0):
            for ich in range(NCH):
                if ich >= 2:
                    P.w(sy, "ff", _ff0 + ich - 1)
                P.inc(sy.dma_start(
                    out=wts[ich % 2][:],
                    in_=wt_scr.ap()[ich * 128:(ich + 1) * 128,
                                    _og * 512:(_og + 1) * 512]), "fw", 16)
        wt_d = P.cnt["fw"] - 16 * NCH

        @block.tensor
        def _(t, _wt_d=wt_d, _ffe=ffe0):
            P.w(t, "nrm", nrm_all)
            P.w(t, "ffe", _ffe)
            ins = None
            for ich in range(NCH):
                P.w(t, "fw", _wt_d + 16 * (ich + 1))
                st, sp = (ich == 0), (ich == NCH - 1)
                for o4 in range(4):
                    lh = wts[ich % 2][:, o4 * 128:(o4 + 1) * 128]
                    t.matmul(pf4[o4][:, 0:512], lh,
                             lnt2[:, ich * 1024:ich * 1024 + 512],
                             start=st, stop=sp)
                    ins = t.matmul(pf4[o4][:, 512:1024], lh,
                                   lnt2[:, ich * 1024 + 512:ich * 1024 + 1024],
                                   start=st, stop=sp)
                P.inc(ins, "ff")
        ff_done = P.cnt["ff"]

        @block.sync
        def _(sy, _og=og, _ffe0=ffe0):
            for o4 in range(4):
                k = _og * 4 + o4
                if k >= 2:
                    P.w(sy, "ffe", k - 1)
                P.inc(sy.dma_start(
                    out=x1n[o4 % 2][:],
                    in_=x1_scr.ap()[k * 128:(k + 1) * 128, :]), "fx", 16)
        x1_d = P.cnt["fx"] - 16 * 4

        @block.vector
        def _(v, _og=og, _ff=ff_done, _x1_d=x1_d):
            P.w(v, "ff", _ff)
            for o4 in range(4):
                k = _og * 4 + o4
                och = k
                P.w(v, "fx", _x1_d + 16 * (o4 + 1))
                if k >= 2:
                    P.w(v, "dA", dA_F0 + 16 * (k - 1))
                ins = v.scalar_tensor_tensor(outb[o4 % 2][:], pf4[o4],
                                             bias_sb[:, och:och + 1],
                                             x1n[o4 % 2][:], ALU.add, ALU.add)
                P.inc(ins, "ffe")
        ffe_n = P.cnt["ffe"]

        @block.scalar
        def _(sc, _og=og, _ffe=ffe_n):
            for o4 in range(4):
                k = _og * 4 + o4
                P.w(sc, "ffe", _ffe - 3 + o4)
                P.inc(sc.dma_start(
                    out=out_fm[k * 128:(k + 1) * 128, :],
                    in_=outb[o4 % 2][:]), "dA", 16)


_DBG_N = [0]


def emit_dbg(nc, block, P, dbg, dram_specs=None, sbuf_specs=None, wait=None,
             sbuf_f32=None, wait_mdve=False, sbuf_extra=None):
    _DBG_N[0] += 1
    with nc.sbuf_tensor(f"dbgf{_DBG_N[0]}", [128, 8192], F32) as dbgf:
        @block.vector
        def _(v):
            P.inc(v.memset(dbgf[:], 0.0), "mDVE")
        md0 = P.cnt["mDVE"]

        if dram_specs:
            with nc.sbuf_tensor(f"dbgb{_DBG_N[0]}", [128, 8192], BF16) as dbgb:
                @block.sync
                def _(sy):
                    P.w(sy, "mDVE", md0)
                    if wait:
                        P.w(sy, wait)
                    for i, (src, row, dt) in enumerate(dram_specs):
                        if dt == BF16:
                            P.inc(sy.dma_start(
                                out=dbgb[:, i * 1024:(i + 1) * 1024],
                                in_=src[row:row + 128, 0:1024]), "dS", 16)
                        else:
                            P.inc(sy.dma_start(
                                out=dbgf[:, i * 1024:(i + 1) * 1024],
                                in_=src[row:row + 128, 0:1024]), "dS", 16)

                @block.vector
                def _(v):
                    P.w(v, "dS")
                    ins = None
                    for i, (src, row, dt) in enumerate(dram_specs):
                        if dt == BF16:
                            ins = v.tensor_copy(dbgf[:, i * 1024:(i + 1) * 1024],
                                                dbgb[:, i * 1024:(i + 1) * 1024])
                    if sbuf_extra:
                        n0 = len(dram_specs)
                        for i, (srcb, off) in enumerate(sbuf_extra):
                            ins = v.tensor_copy(
                                dbgf[:, (n0 + i) * 1024:(n0 + i + 1) * 1024],
                                srcb[:, off:off + 1024])
                    if ins is None:
                        ins = v.engine_nop()
                    P.inc(ins, "mDVE")
        else:
            @block.vector
            def _(v):
                if wait_mdve:
                    P.w(v, "mDVE")
                ins = None
                for i, (src, off) in enumerate(sbuf_specs):
                    w = 1024
                    if src.shape[0] < 128 or src.shape[1] < off + 1024:
                        w = src.shape[1] - off
                    ins = v.tensor_copy(dbgf[:src.shape[0], i * 1024:i * 1024 + w],
                                        src[:src.shape[0], off:off + w])
                P.inc(ins, "mDVE")

        @block.sync
        def _(sy):
            P.w(sy, "mDVE")
            P.inc(sy.dma_start(out=dbg, in_=dbgf[:]), "dS", 16)


def emit_finish(nc, block, P, out_fm):
    with nc.sbuf_tensor("zout", [128, 1024], F32) as z:
        @block.vector
        def _(v):
            P.w(v, "dS")
            P.w(v, "dA")
            P.inc(v.memset(z[:], 7.0), "mDVE")

        @block.sync
        def _(sy):
            P.w(sy, "mDVE")
            for ch in range(NCH):
                P.inc(sy.dma_start(out=out_fm[ch * 128:(ch + 1) * 128, :],
                                   in_=z[:]), "dS", 16)
            P.w(sy, "dS")
            P.w(sy, "dA")
            P.w(sy, "dG")


def kernel(L0_attn_hidden_states, L0_attn_wq, L0_attn_wkv, L0_attn_wo,
           L0_ffn_packed_weight, L0_ffn_scale, L0_ffn_bias):
    x = np.asarray(L0_attn_hidden_states, dtype=np.float32)
    wq = np.asarray(L0_attn_wq, dtype=np.float32)
    wkv = np.asarray(L0_attn_wkv, dtype=np.float32)
    wo_h = np.ascontiguousarray(np.asarray(L0_attn_wo, dtype=np.float32))
    pw_h = np.ascontiguousarray(np.asarray(L0_ffn_packed_weight, dtype=np.int32))
    sc_h = np.ascontiguousarray(np.asarray(L0_ffn_scale, dtype=np.float32))
    fb_h = np.ascontiguousarray(np.asarray(L0_ffn_bias, dtype=np.float32))
    ident = np.eye(128, dtype=ml_dtypes.bfloat16)

    xf = x.reshape(B * S, D)
    in_maps = []
    for c in range(NCORES):
        heads = range(HC * c, HC * c + HC)
        qcols = np.concatenate([np.arange(h * 192, h * 192 + 128) for h in heads])
        kcols = np.concatenate([np.arange(h * 128, h * 128 + 128) for h in heads])
        in_maps.append({
            "x_fm": np.ascontiguousarray(xf[c * TOWN:(c + 1) * TOWN].T),
            "wq_s": np.ascontiguousarray(wq[:, qcols]),
            "wk_s": np.ascontiguousarray(wkv[:, kcols]),
            "wv_s": np.ascontiguousarray(wkv[:, 32 * 128 + kcols]),
            "wo": wo_h, "pw": pw_h, "qsc": sc_h, "fbias": fb_h, "ident": ident,
        })

    nc = build_program(DEBUG_PHASE)
    res = run_bass_kernel_spmd(nc, in_maps, core_ids=list(range(NCORES)))
    out = np.empty((B * S, D), dtype=np.float32)
    for c in range(NCORES):
        out[c * TOWN:(c + 1) * TOWN] = res.results[c]["out_fm"].T
    if DEBUG_PHASE:
        kernel.dbg = [r.get("dbg") for r in res.results]
    return out.reshape(B, S, D)


# version 08
# speedup vs baseline: 1.1889x; 1.1889x over previous
from contextlib import ExitStack

import numpy as np
import ml_dtypes
import concourse.bass as bass
import concourse.mybir as mybir
from concourse.bass_utils import run_bass_kernel_spmd
from concourse.aot_env import allow_ici_all_to_all

F32 = mybir.dt.float32
BF16 = mybir.dt.bfloat16
I32 = mybir.dt.int32
AF = mybir.ActivationFunctionType
ALU = mybir.AluOpType
AXX = mybir.AxisListType.X

NCORES = 8
B = 2
S = 4096
D = 4096
HC = 4
SCALE = 192.0 ** -0.5
NEGC = -0.5 * SCALE
EPS = 1e-5
NCH = D // 128
TOWN = 1024

DEBUG_PHASE = 0
DEQ_CONTIG_TEST = False
CC_DELAY_OPS = 1200


class Prog:
    def __init__(self, nc, es):
        names = ["dS", "dA", "dG", "cc",
                 "mPE", "mACT", "mDVE", "mPOOL",
                 "sq", "smm", "nrm",
                 "wc", "kv", "kve", "kn", "tp", "tpe",
                 "qev", "bia", "qkt", "exp", "av", "msk",
                 "woc", "woe", "wof", "ff", "ffe", "dq", "ccd", "ab",
                 "st", "lxA", "lxB", "bw", "bl", "cw",
                 "ewA", "ewB", "exA", "exB", "fwA", "fwB", "fxA", "fxB",
                 "saA", "saB", "sxA", "sxB", "soA", "soB"]
        self.sem = {}
        self.cnt = {}
        for n in names:
            self.sem[n] = es.enter_context(nc.semaphore("s_" + n))
            self.cnt[n] = 0

    def inc(self, instr, name, n=1):
        instr.then_inc(self.sem[name], n)
        self.cnt[name] += n
        return self.cnt[name]

    def w(self, eng, name, val=None):
        v = self.cnt[name] if val is None else val
        if v > 0:
            eng.wait_ge(self.sem[name], v)


def build_program(debug_phase=0):
    nc = bass.Bass("TRN2", target_bir_lowering=False)

    x_fm = nc.dram_tensor("x_fm", [D, TOWN], F32, kind="ExternalInput").ap()
    wq_s = nc.dram_tensor("wq_s", [D, HC * 128], F32, kind="ExternalInput").ap()
    wk_s = nc.dram_tensor("wk_s", [D, HC * 128], F32, kind="ExternalInput").ap()
    wv_s = nc.dram_tensor("wv_s", [D, HC * 128], F32, kind="ExternalInput").ap()
    wo_in = nc.dram_tensor("wo", [D, D], F32, kind="ExternalInput").ap()
    pw = nc.dram_tensor("pw", [D, 512], I32, kind="ExternalInput").ap()
    qsc = nc.dram_tensor("qsc", [D, 32], F32, kind="ExternalInput").ap()
    fbias = nc.dram_tensor("fbias", [D], F32, kind="ExternalInput").ap()
    ident = nc.dram_tensor("ident", [128, 128], BF16, kind="ExternalInput").ap()

    out_fm = nc.dram_tensor("out_fm", [D, TOWN], F32, kind="ExternalOutput").ap()
    dbg = None
    if debug_phase:
        dbg = nc.dram_tensor("dbg", [128, 8192], F32, kind="ExternalOutput").ap()

    ln_self = nc.dram_tensor("ln_self", [D, TOWN], BF16)
    ln_all = nc.dram_tensor("ln_all", [NCORES * D, TOWN], BF16)
    nc.dram_tensor("pad_lnall", [512, 1024], F32)
    attn_a2a = nc.dram_tensor("attn_a2a", [NCORES * HC * 128, TOWN], BF16)
    attn_own = nc.dram_tensor("attn_own", [NCORES * HC * 128, TOWN], BF16)
    nc.dram_tensor("pad_own", [512, 1024], F32)
    x1_scr = nc.dram_tensor("x1_scr", [D, TOWN], F32)
    w_scr = nc.dram_tensor("w_scr", [D, D], BF16)
    wt_scr = nc.dram_tensor("wt_scr", [D, D], BF16)

    es = ExitStack()
    block = es.enter_context(nc.Block())
    P = Prog(nc, es)
    sbt = lambda *a: es.enter_context(nc.sbuf_tensor(*a))

    ones_col_f = sbt("ones_col_f", [128, 1], F32)
    ones_col_b = sbt("ones_col_b", [128, 1], BF16)
    ones_row_f = sbt("ones_row_f", [1, 128], F32)
    negc_row = sbt("negc_row", [1, 128], F32)
    ident_b = sbt("ident_b", [128, 128], BF16)
    ones_tile = sbt("ones_tile", [128, 512], BF16)
    mask4 = sbt("mask4", [128, 4 * 512], BF16)
    scale_sb = sbt("scale_sb", [128, NCH * 32], F32)
    bias_sb = sbt("bias_sb", [128, NCH], F32)
    kmax_sb = sbt("kmax_sb", [1, HC], F32)
    srow = sbt("srow", [1, 16], F32)
    bias_hh = sbt("bias_hh", [128, 1], F32)
    rec_row = sbt("rec_row", [1, 512], F32)
    recB = sbt("recB", [128, 512], F32)
    def cc_quiet():
        cc_now = P.cnt["cc"]

        @block.vector
        def _(v, _cc=cc_now):
            P.w(v, "cc", _cc)
            ins = None
            for _ in range(CC_DELAY_OPS):
                ins = v.tensor_copy(ones_tile[:, 0:256], ones_tile[:, 256:512])
            if ins is None:
                ins = v.engine_nop()
            P.inc(ins, "ccd")

    @block.vector
    def _(v):
        v.memset(ones_col_f[:], 1.0)
        v.memset(ones_col_b[:], 1.0)
        v.memset(ones_row_f[:], 1.0)
        v.memset(negc_row[:], NEGC)
        P.inc(v.memset(ones_tile[:], 1.0), "mDVE")

    @block.sync
    def _(sy):
        P.inc(sy.dma_start(out=ident_b[:], in_=ident), "st", 16)
        for och in range(NCH):
            P.inc(sy.dma_start(out=scale_sb[:, och * 32:(och + 1) * 32],
                               in_=qsc[och * 128:(och + 1) * 128, 0:32]), "st", 16)
            P.inc(sy.dma_start(out=bias_sb[:, och:och + 1],
                               in_=fbias[och * 128:(och + 1) * 128]), "st", 16)

    @block.gpsimd
    def _(g):
        P.w(g, "mDVE")
        for r in range(4):
            ins = g.affine_select(mask4[:, r * 512:(r + 1) * 512], ones_tile[:],
                                  pattern=[[1, 512]], compare_op=ALU.is_ge,
                                  fill=0.0, base=-(r * 128), channel_multiplier=-1)
        P.inc(ins, "mPOOL")

    if debug_phase == 6:
        with nc.sbuf_tensor("m6a", [128, 1024], F32) as m6a, \
             nc.sbuf_tensor("m6b", [128, 1024], BF16) as m6b:
            @block.sync
            def _(sy):
                for ch in range(NCH):
                    P.inc(sy.dma_start(out=m6a[:],
                                       in_=x_fm[ch * 128:(ch + 1) * 128, :]), "dS", 16)
                    P.w(sy, "dS")
                    P.w(sy, "nrm")

                    @block.vector
                    def _(v):
                        P.w(v, "dS")
                        P.inc(v.tensor_copy(m6b[:], m6a[:]), "nrm")

                    P.w(sy, "nrm")
                    P.inc(sy.dma_start(
                        out=ln_self.ap()[ch * 128:(ch + 1) * 128, :],
                        in_=m6b[:]), "dS", 16)
                    P.w(sy, "dS")

        @block.gpsimd
        def _(g):
            P.w(g, "dS")
            ins = g.collective_compute(
                "AllGather", ALU.bypass, replica_groups=[list(range(NCORES))],
                ins=[ln_self.ap().opt()], outs=[ln_all.ap().opt()])
            P.inc(ins, "cc")

        emit_dbg(nc, block, P, dbg, dram_specs=[(ln_self.ap(), 0, BF16),
                                                (ln_all.ap(), 0, BF16),
                                                (ln_all.ap(), 4 * D, BF16)],
                 wait="cc")
        emit_finish(nc, block, P, out_fm)
        es.close()
        return nc

    if debug_phase in (1, 5):
        with nc.sbuf_tensor("canary", [128, 1024], BF16) as cnr:
            @block.vector
            def _(v):
                P.inc(v.memset(cnr[:], 3.0), "mDVE")

            @block.sync
            def _(sy):
                P.w(sy, "mDVE")
                for ch in range(NCH):
                    P.inc(sy.dma_start(
                        out=ln_self.ap()[ch * 128:(ch + 1) * 128, :],
                        in_=cnr[:]), "dS", 16)

    with nc.sbuf_tensor("lnbuf", [128, NCH * 1024], BF16) as lnbuf, \
         nc.sbuf_tensor("strA", [128, 2 * 1024], F32) as strA, \
         nc.sbuf_tensor("xsqA", [128, 2 * 1024], BF16) as xsqA, \
         nc.sbuf_tensor("t1A", [128, 1024], F32) as t1A, \
         nc.sbuf_tensor("rnA", [128, 2048], F32) as rnA, \
         nc.sbuf_tensor("mrA", [1, 4 * 1024], F32) as mrA, \
         nc.psum_tensor("pstA", [128, 2048], F32) as pstA, \
         nc.psum_tensor("pbcA", [128, 2048], F32) as pbcA:

        emit_ln(nc, block, P, x_src=x_fm, out_bf=lnbuf, stream_buf=strA,
                xsq=xsqA, t1buf=t1A, rnB=rnA, pstat=pstA, pbc=pbcA,
                m_rows=mrA, ones_col_f=ones_col_f, ones_row_f=ones_row_f,
                ones_col_b=ones_col_b)

        if debug_phase in (9,):
            emit_dbg(nc, block, P, dbg,
                     sbuf_specs=[(lnbuf, 0), (rnA, 0), (rnA, 1024), (strA, 0)],
                     sbuf_f32=[False, True, True, True], wait_mdve=True)

        with nc.sbuf_tensor("stg", [128, 2048], BF16) as stg:
            for ch in range(NCH):
                mdve_c = P.cnt["mDVE"]
                dS_c = P.cnt["dS"]

                @block.vector
                def _(v, _ch=ch, _d=dS_c):
                    P.w(v, "dS", _d)
                    ins = v.tensor_copy(stg[:, (_ch % 2) * 1024:(_ch % 2) * 1024 + 1024],
                                        lnbuf[:, _ch * 1024:(_ch + 1) * 1024])
                    P.inc(ins, "mDVE")
                mdve_n = P.cnt["mDVE"]

                @block.sync
                def _(sy, _ch=ch, _m=mdve_n):
                    P.w(sy, "mDVE", _m)
                    P.inc(sy.dma_start(
                        out=ln_self.ap()[_ch * 128:(_ch + 1) * 128, :],
                        in_=stg[:, (_ch % 2) * 1024:(_ch % 2) * 1024 + 1024]), "dS", 16)

        if debug_phase == 5:
            with nc.sbuf_tensor("canary5", [128, 1024], BF16) as cnr5:
                @block.vector
                def _(v):
                    P.w(v, "nrm")
                    P.inc(v.memset(cnr5[:], 3.0), "mDVE")

                @block.sync
                def _(sy):
                    P.w(sy, "mDVE")
                    for ch in range(NCH):
                        P.inc(sy.dma_start(
                            out=ln_self.ap()[ch * 128:(ch + 1) * 128, :],
                            in_=cnr5[:]), "dS", 16)

    if debug_phase == 9:
        emit_finish(nc, block, P, out_fm)
        es.close()
        return nc

    with nc.sbuf_tensor("flsh", [128, 1024], BF16) as flsh:
        @block.sync
        def _(sy):
            P.w(sy, "dS")
            P.inc(sy.dma_start(out=flsh[:], in_=ln_self.ap()[0:128, :]), "dS", 16)

    @block.gpsimd
    def _(g):
        P.w(g, "dS")
        P.w(g, "dA")
        P.w(g, "dG")
        ins = g.collective_compute(
            "AllGather", ALU.bypass, replica_groups=[list(range(NCORES))],
            ins=[ln_self.ap().opt()], outs=[ln_all.ap().opt()])
        P.inc(ins, "cc")

    if debug_phase in (1, 5):
        emit_dbg(nc, block, P, dbg, dram_specs=[(ln_self.ap(), 0, BF16),
                                                (ln_self.ap(), 2048, BF16),
                                                (ln_all.ap(), 0, BF16),
                                                (ln_all.ap(), 4 * D, BF16)],
                 wait="cc")
        emit_finish(nc, block, P, out_fm)
        es.close()
        return nc

    dq_chunks = []
    if debug_phase not in (2, 21, 3):
        dq_bufs = (es.enter_context(nc.sbuf_tensor("pkq", [128, 512], I32)),
                   es.enter_context(nc.sbuf_tensor("shq", [128, 512], I32)),
                   es.enter_context(nc.sbuf_tensor("dqq", [128, 4096], BF16)))
        dq_chunks = make_dequant_chunks(nc, block, P, pw, scale_sb,
                                        dq_bufs[0], dq_bufs[1], dq_bufs[2],
                                        w_scr)

    with nc.sbuf_tensor("K4", [128, HC * S], BF16) as K4, \
         nc.sbuf_tensor("v4", [128, HC * S], BF16) as v4, \
         nc.sbuf_tensor("lnt", [128, 16 * 1024], BF16) as lnt, \
         nc.sbuf_tensor("qbuf4", [128, HC * 1024], BF16) as qbuf4, \
         nc.sbuf_tensor("qsqb", [128, 2048], BF16) as qsqb, \
         nc.sbuf_tensor("prb0", [128, 1024], BF16) as prb0, \
         nc.sbuf_tensor("prb1", [128, 1024], BF16) as prb1, \
         nc.sbuf_tensor("att0", [128, 512], BF16) as att0, \
         nc.sbuf_tensor("att1", [128, 512], BF16) as att1:

        probt = [prb0, prb1]
        attn_b = [att0, att1]

        for b in range(B):
            with nc.sbuf_tensor(f"wkb{b}", [128, NCH * 512], BF16) as wkb, \
                 nc.sbuf_tensor(f"wvb{b}", [128, NCH * 512], BF16) as wvb, \
                 nc.sbuf_tensor(f"wldB{b}", [128, 512], F32) as wldB, \
                 nc.psum_tensor(f"PK{b}", [128, 2048], F32) as PK, \
                 nc.psum_tensor(f"PV{b}", [128, 2048], F32) as PV:
                emit_phase_B(nc, block, P, b, ln_all, wk_s, wv_s, K4, v4, lnt,
                             wkb, wvb, wldB, PK, PV, qsqb, kmax_sb, srow,
                             ones_col_b,
                             bg=(dq_chunks[:16] if b == 0 else []),
                             post_cast=(cc_quiet if b == 0 else None))

            if debug_phase == 21 and b == 0:
                @block.vector
                def _(v):
                    P.w(v, "kve")
                    P.w(v, "kn")
                    P.inc(v.engine_nop(), "mDVE")
                emit_dbg(nc, block, P, dbg,
                         sbuf_specs=[(K4, 0), (K4, 4096), (v4, 0), (K4, 2048)],
                         wait_mdve=True)
            if debug_phase == 21:
                continue

            with nc.sbuf_tensor(f"wqb{b}", [128, NCH * 512], BF16) as wqb, \
                 nc.sbuf_tensor(f"wldC{b}", [128, 512], F32) as wldC:
                with nc.psum_tensor(f"ptp{b}", [128, 2048], BF16) as ptp:
                    emit_phase_Cpre(nc, block, P, b, v4, ident_b, ptp,
                                    wq_s, wqb, wldC)

                if debug_phase == 2 and b == 0:
                    @block.vector
                    def _(v):
                        P.w(v, "tpe")
                        P.w(v, "wc")
                        P.w(v, "kve")
                        P.inc(v.engine_nop(), "mDVE")
                    emit_dbg(nc, block, P, dbg,
                             sbuf_specs=[(K4, 0), (K4, 4096), (v4, 0), (wqb, 0)], wait_mdve=True)

                if debug_phase in (2,):
                    continue

                bg = dq_chunks[16:] if b == 0 else []
                with nc.psum_tensor(f"psc{b}", [128, 2048], F32) as psc, \
                     nc.psum_tensor(f"psc2{b}", [128, 2048], F32) as psc2:
                    emit_phase_C(nc, block, P, b, ln_all, attn_a2a, K4, v4,
                                 lnt, wqb, qbuf4, qsqb, probt, attn_b, mask4,
                                 bias_hh, kmax_sb, srow, ones_col_b,
                                 ones_row_f, negc_row, rec_row, recB,
                                 psc, psc2, bg=bg)


        if debug_phase not in (2, 21, 3, 4):
            with nc.sbuf_tensor("wttq", [128, 4096], BF16) as wttq:
                emit_w_transpose(nc, block, P, w_scr, wt_scr, wttq)

    if debug_phase in (2, 21):
        emit_finish(nc, block, P, out_fm)
        es.close()
        return nc

    if debug_phase == 3:
        emit_dbg(nc, block, P, dbg,
                 dram_specs=[(attn_a2a.ap(), 0, BF16),
                             (attn_a2a.ap(), 128, BF16),
                             (attn_a2a.ap(), 512, BF16),
                             (attn_a2a.ap(), 2048, BF16)],
                 wait="dA")
        emit_finish(nc, block, P, out_fm)
        es.close()
        return nc

    @block.gpsimd
    def _(g):
        P.w(g, "dS")
        P.w(g, "dA")
        P.w(g, "saA")
        P.w(g, "saB")
        P.w(g, "dG")
        P.w(g, "dq")
        with allow_ici_all_to_all():
            ins = g.collective_compute(
                "AllToAll", ALU.bypass, replica_groups=[list(range(NCORES))],
                ins=[attn_a2a.ap().opt()], outs=[attn_own.ap().opt()])
        P.inc(ins, "cc")

    cc_quiet()

    if debug_phase == 40:
        emit_dbg(nc, block, P, dbg, dram_specs=[(attn_own.ap(), 0, BF16),
                                                (attn_own.ap(), 512, BF16),
                                                (attn_own.ap(), 2048, BF16),
                                                (attn_own.ap(), 3584, BF16)],
                 wait="cc")
        emit_finish(nc, block, P, out_fm)
        es.close()
        return nc

    abuf4dump = None
    x1snap = None
    absnap = None
    if debug_phase == 4:
        x1snap = es.enter_context(nc.sbuf_tensor("x1snap", [128, 1024], F32))
    if debug_phase == 52:
        absnap = es.enter_context(nc.sbuf_tensor("absnap", [128, 2048], BF16))
    with nc.sbuf_tensor("abuf", [128, NCH * 1024], BF16) as abuf, \
         nc.sbuf_tensor("wldE0", [128, 512], F32) as wldE0, \
         nc.sbuf_tensor("wldE1", [128, 512], F32) as wldE1, \
         nc.sbuf_tensor("wcb0", [128, 512], BF16) as wcb0, \
         nc.sbuf_tensor("wcb1", [128, 512], BF16) as wcb1, \
         nc.sbuf_tensor("xch0", [128, 1024], F32) as xch0, \
         nc.sbuf_tensor("xch1", [128, 1024], F32) as xch1, \
         nc.sbuf_tensor("x1c0", [128, 1024], F32) as x1c0, \
         nc.sbuf_tensor("x1c1", [128, 1024], F32) as x1c1, \
         nc.psum_tensor("pwoA", [128, 2048], F32) as pwoA, \
         nc.psum_tensor("pwoB", [128, 2048], F32) as pwoB:
        emit_phase_E(nc, block, P, attn_own, wo_in, x_fm, x1_scr, abuf,
                     [wldE0, wldE1], [wcb0, wcb1], [xch0, xch1],
                     [x1c0, x1c1], pwoA, pwoB, x1snap=x1snap, absnap=absnap,
                     recB=recB)
        abuf4dump = abuf
        if debug_phase == 4:
            emit_dbg(nc, block, P, dbg,
                     dram_specs=[(x1_scr.ap(), 0, F32),
                                 (x1_scr.ap(), 512, F32),
                                 (attn_own.ap(), 0, BF16)],
                 wait="dA", sbuf_extra=[(x1snap, 0), (abuf, 0),
                                        (abuf, 2048), (abuf, 31 * 1024)])

    if debug_phase == 4:
        emit_finish(nc, block, P, out_fm)
        es.close()
        return nc

    if debug_phase == 50:
        @block.sync
        def _(sy):
            P.w(sy, "dS")
            P.w(sy, "dA")
        emit_dbg(nc, block, P, dbg, dram_specs=[(w_scr.ap(), 0, BF16),
                                                (w_scr.ap(), 2048, BF16),
                                                (wt_scr.ap(), 0, BF16),
                                                (wt_scr.ap(), 2048, BF16)],
                 wait="dG")
        emit_finish(nc, block, P, out_fm)
        es.close()
        return nc

    with nc.sbuf_tensor("lnt2", [128, NCH * 1024], BF16) as lnt2, \
         nc.sbuf_tensor("strF", [128, 2 * 1024], F32) as strF, \
         nc.sbuf_tensor("xsqF", [128, 2 * 1024], BF16) as xsqF, \
         nc.sbuf_tensor("t1F", [128, 1024], F32) as t1F, \
         nc.sbuf_tensor("rnF", [128, 2048], F32) as rnF, \
         nc.sbuf_tensor("mrF", [1, 4 * 1024], F32) as mrF:

        with nc.psum_tensor("pstF", [128, 2048], F32) as pstF, \
             nc.psum_tensor("pbcF", [128, 2048], F32) as pbcF:
            emit_ln(nc, block, P, x_src=x1_scr.ap(), out_bf=lnt2,
                    stream_buf=strF, xsq=xsqF, t1buf=t1F, rnB=rnF,
                    pstat=pstF, pbc=pbcF, m_rows=mrF,
                    ones_col_f=ones_col_f, ones_row_f=ones_row_f,
                    ones_col_b=ones_col_b, wait_sx_first=True)

        if debug_phase == 51:
            emit_dbg(nc, block, P, dbg,
                     sbuf_specs=[(lnt2, 0), (lnt2, 16 * 1024), (lnt2, 31 * 1024)],
                     wait_mdve=True)

        with nc.sbuf_tensor("wts0", [128, 512], BF16) as wts0, \
             nc.sbuf_tensor("wts1", [128, 512], BF16) as wts1, \
             nc.sbuf_tensor("x1n0", [128, 1024], F32) as x1n0, \
             nc.sbuf_tensor("x1n1", [128, 1024], F32) as x1n1, \
             nc.sbuf_tensor("outb0", [128, 1024], F32) as outb0, \
             nc.sbuf_tensor("outb1", [128, 1024], F32) as outb1, \
             nc.psum_tensor("pfA", [128, 2048], F32) as pfA, \
             nc.psum_tensor("pfB", [128, 2048], F32) as pfB:
            if debug_phase != 51:
                emit_phase_F(nc, block, P, wt_scr, x1_scr, lnt2, bias_sb,
                             out_fm, [wts0, wts1], [x1n0, x1n1],
                             [outb0, outb1], pfA, pfB)

    if debug_phase == 51:
        emit_finish(nc, block, P, out_fm)
        es.close()
        return nc

    if debug_phase == 52:
        @block.sync
        def _(sy):
            P.w(sy, "dA")
        emit_dbg(nc, block, P, dbg, dram_specs=[(x1_scr.ap(), 0, F32),
                                                (x1_scr.ap(), 512, F32),
                                                (attn_own.ap(), 0, BF16),
                                                (attn_own.ap(), 2048, BF16)],
                 wait="dA", sbuf_extra=[(absnap, 0), (absnap, 1024)])

    @block.sync
    def _(sy):
        for s in ("dS", "dA", "dG", "st", "bw", "bl", "cw",
                  "lxA", "lxB", "ewA", "ewB", "exA", "exB",
                  "fwA", "fwB", "fxA", "fxB",
                  "saA", "saB", "sxA", "sxB", "soA", "soB", "ab"):
            P.w(sy, s)

    es.close()
    return nc


def emit_ln(nc, block, P, x_src, out_bf, stream_buf, xsq, t1buf, rnB,
            pstat, pbc, m_rows, ones_col_f, ones_row_f, ones_col_b=None,
            wait_sx_first=False):
    sq0, smm0, nrm0 = P.cnt["sq"], P.cnt["smm"], P.cnt["nrm"]
    sx_now = (P.cnt["sxA"], P.cnt["sxB"])
    lx0 = (P.cnt["lxA"], P.cnt["lxB"])

    def lxs(ch):
        return "lxA" if ch % 2 == 0 else "lxB"

    @block.sync
    def _(sy):
        if wait_sx_first:
            P.w(sy, "sxA", sx_now[0])
            P.w(sy, "sxB", sx_now[1])
        for ch in range(NCH):
            if ch >= 2:
                P.w(sy, "smm", smm0 + ch - 1)
            P.inc(sy.dma_start(
                out=stream_buf[:, (ch % 2) * 1024:(ch % 2) * 1024 + 1024],
                in_=x_src[ch * 128:(ch + 1) * 128, :]), lxs(ch), 16)

    def cfull(ch):
        return stream_buf[:, (ch % 2) * 1024:(ch % 2) * 1024 + 1024]

    @block.scalar
    def _(sc):
        for ch in range(NCH):
            P.w(sc, lxs(ch), lx0[ch % 2] + 16 * (ch // 2 + 1))
            if ch >= 2:
                P.w(sc, "smm", smm0 + ch - 1)
            ins = sc.activation(xsq[:, (ch % 2) * 1024:(ch % 2) * 1024 + 1024],
                                cfull(ch), AF.Square)
            P.inc(ins, "sq")

    @block.tensor
    def _(t):
        P.w(t, "mDVE", 1)
        for ch in range(NCH):
            P.w(t, "sq", sq0 + ch + 1)
            st, sp = (ch == 0), (ch == NCH - 1)
            c = cfull(ch)
            t.matmul(pstat[0:1, 0:512], ones_col_f[:], c[:, 0:512], start=st, stop=sp)
            t.matmul(pstat[0:1, 512:1024], ones_col_f[:], c[:, 512:1024], start=st, stop=sp)
            t.matmul(pstat[0:1, 1024:1536], ones_col_b[:],
                     xsq[:, (ch % 2) * 1024:(ch % 2) * 1024 + 512], start=st, stop=sp)
            ins = t.matmul(pstat[0:1, 1536:2048], ones_col_b[:],
                           xsq[:, (ch % 2) * 1024 + 512:(ch % 2) * 1024 + 1024],
                           start=st, stop=sp)
            P.inc(ins, "smm")

    m = m_rows[0:1, 0:1024]
    var = m_rows[0:1, 1024:2048]
    rstd = m_rows[0:1, 2048:3072]
    negm = m_rows[0:1, 3072:4096]

    @block.vector
    def _(v):
        P.w(v, "smm", smm0 + NCH)
        v.tensor_scalar(m, pstat[0:1, 0:1024], 1.0 / D, None, ALU.mult)
        v.tensor_scalar(var, pstat[0:1, 1024:2048], 1.0 / D, None, ALU.mult)
        v.tensor_tensor(rstd, m, m, ALU.mult)
        v.tensor_sub(var, var, rstd)
        ins = v.tensor_scalar_add(var, var, EPS)
        P.inc(ins, "mDVE")

    @block.scalar
    def _(sc):
        P.w(sc, "mDVE")
        P.inc(sc.activation(rstd, var, AF.Sqrt), "mACT")

    @block.vector
    def _(v):
        P.w(v, "mACT")
        v.reciprocal(var, rstd)
        v.tensor_copy(rstd, var)
        ins = v.scalar_tensor_tensor(negm, m, -1.0, rstd, ALU.mult, ALU.mult)
        P.inc(ins, "mDVE")

    @block.tensor
    def _(t):
        P.w(t, "mDVE")
        t.matmul(pbc[:, 0:512], ones_row_f[:], rstd[0:1, 0:512], start=True, stop=True)
        t.matmul(pbc[:, 512:1024], ones_row_f[:], rstd[0:1, 512:1024], start=True, stop=True)
        t.matmul(pbc[:, 1024:1536], ones_row_f[:], negm[0:1, 0:512], start=True, stop=True)
        ins = t.matmul(pbc[:, 1536:2048], ones_row_f[:], negm[0:1, 512:1024],
                       start=True, stop=True)
        P.inc(ins, "mPE")

    @block.vector
    def _(v):
        P.w(v, "mPE")
        v.tensor_copy(rnB[:, 0:1024], pbc[:, 0:1024])
        ins = v.tensor_copy(rnB[:, 1024:2048], pbc[:, 1024:2048])
        P.inc(ins, "mDVE")
    rn_done = P.cnt["mDVE"]

    lx1 = (P.cnt["lxA"], P.cnt["lxB"])

    @block.sync
    def _(sy):
        P.w(sy, "mDVE", rn_done)
        for ch in range(NCH):
            if ch >= 2:
                P.w(sy, "nrm", nrm0 + ch - 1)
            P.inc(sy.dma_start(
                out=stream_buf[:, (ch % 2) * 1024:(ch % 2) * 1024 + 1024],
                in_=x_src[ch * 128:(ch + 1) * 128, :]), lxs(ch), 16)

    @block.vector
    def _(v):
        for ch in range(NCH):
            P.w(v, lxs(ch), lx1[ch % 2] + 16 * (ch // 2 + 1))
            v.tensor_tensor(t1buf[:], cfull(ch), rnB[:, 0:1024], ALU.mult)
            ins = v.tensor_tensor(out_bf[:, ch * 1024:(ch + 1) * 1024],
                                  t1buf[:], rnB[:, 1024:2048], ALU.add)
            P.inc(ins, "nrm")
        P.inc(v.engine_nop(), "mDVE")


def emit_phase_B(nc, block, P, b, ln_all, wk_s, wv_s, K4, v4, lnt,
                 wkb, wvb, wldB, PK, PV, ksqb, kmax_sb, srow, ones_col_b,
                 bg=(), post_cast=None):
    bg = list(bg)
    wc0 = P.cnt["wc"]

    @block.sync
    def _(sy):
        P.w(sy, "dS")
        for ch in range(2 * NCH):
            if ch >= 1:
                P.w(sy, "wc", wc0 + ch)
            w_src = wk_s if ch % 2 == 0 else wv_s
            c = ch // 2
            P.inc(sy.dma_start(out=wldB[:],
                               in_=w_src[c * 128:(c + 1) * 128, :]), "bw", 16)
    wdl = P.cnt["bw"] - 32 * NCH

    @block.vector
    def _(v):
        for ch in range(2 * NCH):
            P.w(v, "bw", wdl + 16 * (ch + 1))
            dst = wkb if ch % 2 == 0 else wvb
            c = ch // 2
            ins = v.tensor_copy(dst[:, c * 512:(c + 1) * 512], wldB[:])
            P.inc(ins, "wc")
    wc_done = P.cnt["wc"]
    if post_cast is not None:
        post_cast()

    for tb in range(4):
        rank = 4 * b + tb
        for hp in range(2):
            kve_before = P.cnt["kve"]
            for h2 in range(2):
                kv_prev = P.cnt["kv"]
                qev_prev = P.cnt["qev"]
                cc_now = P.cnt["cc"]

                @block.sync
                def _(sy, _rank=rank, _h2=h2, _kv=kv_prev, _qev=qev_prev,
                      _cc=cc_now):
                    P.w(sy, "cc", _cc)
                    P.w(sy, "ccd", min(_cc, P.cnt["ccd"]))
                    P.w(sy, "kv", _kv)
                    P.w(sy, "qev", _qev)
                    r0 = _rank * D + _h2 * 16 * 128
                    P.inc(sy.dma_start(
                        out=lnt[:].rearrange("p (c t) -> p c t", t=1024),
                        in_=ln_all.ap()[r0:r0 + 16 * 128, :]
                        .rearrange("(c p) t -> p c t", p=128)), "bl", 16)
                lnt_d = P.cnt["bl"]

                @block.tensor
                def _(t, _hp=hp, _h2=h2, _lnt_d=lnt_d, _kve=kve_before):
                    P.w(t, "bl", _lnt_d)
                    if _h2 == 0:
                        P.w(t, "kve", _kve)
                    ins = None
                    for cc16 in range(16):
                        ch = _h2 * 16 + cc16
                        P.w(t, "wc", min(wc_done, wc0 + 2 * (ch + 1)))
                        st, sp = (ch == 0), (ch == NCH - 1)
                        for hh in range(2):
                            h = 2 * _hp + hh
                            wkt = wkb[:, ch * 512 + h * 128:
                                      ch * 512 + h * 128 + 128]
                            wvt = wvb[:, ch * 512 + h * 128:
                                      ch * 512 + h * 128 + 128]
                            for half in range(2):
                                rhs = lnt[:, cc16 * 1024 + half * 512:
                                          cc16 * 1024 + half * 512 + 512]
                                sl = slice((hh * 2 + half) * 512,
                                           (hh * 2 + half + 1) * 512)
                                t.matmul(PK[:, sl], wkt, rhs, start=st, stop=sp)
                                ins = t.matmul(PV[:, sl], wvt, rhs,
                                               start=st, stop=sp)
                    P.inc(ins, "kv")

            @block.scalar
            def _(sc, _hp=hp, _tb=tb):
                P.w(sc, "kv")
                ins = None
                for hh in range(2):
                    h = 2 * _hp + hh
                    for half in range(2):
                        off = h * S + _tb * 1024 + half * 512
                        sl = slice((hh * 2 + half) * 512, (hh * 2 + half + 1) * 512)
                        sc.activation(K4[:, off:off + 512], PK[:, sl], AF.Copy)
                        ins = sc.activation(v4[:, off:off + 512], PV[:, sl], AF.Copy)
                P.inc(ins, "kve")

            for _ in range(2):
                if bg:
                    bg.pop(0)()

    kn0 = P.cnt["kn"]
    kve_all = P.cnt["kve"]
    mdve0 = P.cnt["mDVE"]
    sqk0 = P.cnt["sq"]

    @block.scalar
    def _(sc):
        P.w(sc, "kve", kve_all)
        for i in range(HC * 8):
            if i >= 1:
                P.w(sc, "kn", kn0 + i)
            h, j = divmod(i, 8)
            ins = sc.activation(ksqb[:, 0:512],
                                K4[:, h * S + j * 512:h * S + (j + 1) * 512],
                                AF.Square)
            P.inc(ins, "sq")

    @block.tensor
    def _(t):
        for i in range(HC * 8):
            P.w(t, "sq", sqk0 + i + 1)
            if i >= 1:
                P.w(t, "mDVE", mdve0 + i)
            ins = t.matmul(PK[0:1, 0:512], ones_col_b[:], ksqb[:, 0:512],
                           start=True, stop=True)
            P.inc(ins, "kn")

    @block.vector
    def _(v):
        for i in range(HC * 8):
            h, j = divmod(i, 8)
            P.w(v, "kn", kn0 + i + 1)
            if j == 0:
                ins = v.tensor_reduce(kmax_sb[0:1, h:h + 1], PK[0:1, 0:512],
                                      AXX, ALU.max)
            else:
                v.tensor_reduce(srow[0:1, 0:1], PK[0:1, 0:512], AXX, ALU.max)
                ins = v.tensor_tensor(kmax_sb[0:1, h:h + 1],
                                      kmax_sb[0:1, h:h + 1],
                                      srow[0:1, 0:1], ALU.max)
            P.inc(ins, "mDVE")


def emit_phase_Cpre(nc, block, P, b, v4, ident_b, ptp, wq_s, wqb, wldC):
    tp0, tpe0 = P.cnt["tp"], P.cnt["tpe"]
    kve_all = P.cnt["kve"]
    mdve_all = P.cnt["mDVE"]

    @block.tensor
    def _(t):
        P.w(t, "st")
        P.w(t, "kve", kve_all)
        P.w(t, "mDVE", mdve_all)
        for i in range(HC * 32):
            h, kt = divmod(i, 32)
            if i >= 2:
                P.w(t, "tpe", tpe0 + i - 1)
            off = h * S + kt * 128
            ins = t.transpose(ptp[:, (i % 2) * 1024:(i % 2) * 1024 + 128],
                              v4[:, off:off + 128], ident_b[:])
            P.inc(ins, "tp")

    @block.scalar
    def _(sc):
        for i in range(HC * 32):
            h, kt = divmod(i, 32)
            P.w(sc, "tp", tp0 + i + 1)
            off = h * S + kt * 128
            ins = sc.activation(v4[:, off:off + 128],
                                ptp[:, (i % 2) * 1024:(i % 2) * 1024 + 128], AF.Copy)
            P.inc(ins, "tpe")

    wc0 = P.cnt["wc"]

    @block.sync
    def _(sy):
        for ch in range(NCH):
            if ch >= 1:
                P.w(sy, "wc", wc0 + ch)
            P.inc(sy.dma_start(out=wldC[:],
                               in_=wq_s[ch * 128:(ch + 1) * 128, :]), "cw", 16)
    wql = P.cnt["cw"] - 16 * NCH

    @block.vector
    def _(v):
        for ch in range(NCH):
            P.w(v, "cw", wql + 16 * (ch + 1))
            ins = v.tensor_copy(wqb[:, ch * 512:(ch + 1) * 512], wldC[:])
            P.inc(ins, "wc")


def emit_phase_C(nc, block, P, b, ln_all, attn_a2a, K4, v4, lnt, wqb,
                 qbuf4, qsqb, probt, attn_b, mask4, bias_hh, kmax_sb, srow,
                 ones_col_b, ones_row_f, negc_row, rec_row, recB,
                 psc, psc2, bg=()):
    bg = list(bg)
    """Attention for 4 heads of batch b over all queries (causal).
    PSUM roles: qproj -> pq4[h] = (psc|psc2)[:, (h%2)*1024:+1024];
    attention -> psc = 4-deep score ring [128,512] slots,
    pav = psc2[:, 0:512], ps_s = psc2[:, 512:1024],
    bias bcast = psc[:, 1024:1025], recB bcast = psc[:, 0:512]."""
    wq_done = P.cnt["wc"]
    tpe_done = P.cnt["tpe"]

    @block.vector
    def _(v):
        ins = v.tensor_reduce(srow[0:1, 3:4], kmax_sb[0:1, 0:HC], AXX, ALU.max)
        P.inc(ins, "mDVE")

    pav = psc2[:, 0:512]
    ps_s = psc2[0:1, 512:1024]
    pq4 = [psc[:, 0:1024], psc[:, 1024:2048],
           psc2[:, 0:1024], psc2[:, 1024:2048]]

    for qb in range(4):
        rank = 4 * b + qb
        for h2 in range(2):
            kv_prev = P.cnt["kv"]
            cc_now = P.cnt["cc"]

            @block.sync
            def _(sy, _rank=rank, _h2=h2, _kv=kv_prev, _cc=cc_now):
                P.w(sy, "cc", _cc)
                P.w(sy, "ccd", min(_cc, P.cnt["ccd"]))
                P.w(sy, "kv", _kv)
                r0 = _rank * D + _h2 * 16 * 128
                P.inc(sy.dma_start(
                    out=lnt[:].rearrange("p (c t) -> p c t", t=1024),
                    in_=ln_all.ap()[r0:r0 + 16 * 128, :]
                    .rearrange("(c p) t -> p c t", p=128)), "bl", 16)
            lnt_d = P.cnt["bl"]

            exp_all = P.cnt["exp"]
            mdve_all = P.cnt["mDVE"]
            bia_all = P.cnt["bia"]
            qev_all = P.cnt["qev"]

            @block.tensor
            def _(t, _h2=h2, _lnt_d=lnt_d, _exp=exp_all, _mdve=mdve_all,
                  _bia=bia_all, _qev=qev_all):
                P.w(t, "bl", _lnt_d)
                P.w(t, "wc", wq_done)
                P.w(t, "tpe", tpe_done)
                if _h2 == 0:
                    P.w(t, "exp", _exp)
                    P.w(t, "mDVE", _mdve)
                    P.w(t, "bia", _bia)
                    P.w(t, "qev", _qev)
                ins = None
                for cc16 in range(16):
                    ch = _h2 * 16 + cc16
                    st, sp = (ch == 0), (ch == NCH - 1)
                    for h in range(HC):
                        wqt = wqb[:, ch * 512 + h * 128: ch * 512 + h * 128 + 128]
                        t.matmul(pq4[h][:, 0:512], wqt,
                                 lnt[:, cc16 * 1024:cc16 * 1024 + 512],
                                 start=st, stop=sp)
                        ins = t.matmul(pq4[h][:, 512:1024], wqt,
                                       lnt[:, cc16 * 1024 + 512:cc16 * 1024 + 1024],
                                       start=st, stop=sp)
                P.inc(ins, "kv")
        kv_qp = P.cnt["kv"]

        @block.scalar
        def _(sc, _kv=kv_qp):
            P.w(sc, "kv", _kv)
            ins = None
            for h in range(HC):
                ins = sc.activation(qbuf4[:, h * 1024:(h + 1) * 1024],
                                    pq4[h], AF.Copy)
            P.inc(ins, "qev")
        qev_done = P.cnt["qev"]

        sq_b = P.cnt["sq"]
        exp_b = P.cnt["exp"]
        mdve_b = P.cnt["mDVE"]

        for half4 in range(2):
            @block.scalar
            def _(sc, _qev=qev_done, _h4=half4):
                if _h4 == 0:
                    P.w(sc, "qev", _qev)
                else:
                    P.w(sc, "mPE", P.cnt["mPE"])
                ins = sc.activation(qsqb[:],
                                    qbuf4[:, _h4 * 2048:(_h4 + 1) * 2048],
                                    AF.Square)
                P.inc(ins, "sq")
            sq_n = P.cnt["sq"]

            @block.tensor
            def _(t, _sq=sq_n, _exp=exp_b, _mdve=mdve_b, _h4=half4):
                P.w(t, "sq", _sq)
                if _h4 == 0:
                    P.w(t, "exp", _exp)
                    P.w(t, "mDVE", _mdve)
                tgt = psc if _h4 == 0 else psc2
                ins = None
                for j in range(4):
                    ins = t.matmul(tgt[0:1, j * 512:(j + 1) * 512],
                                   ones_col_b[:],
                                   qsqb[:, j * 512:(j + 1) * 512],
                                   start=True, stop=True)
                P.inc(ins, "mPE")
        mpe_b = P.cnt["mPE"]

        @block.vector
        def _(v, _mpe=mpe_b, _b=b):
            P.w(v, "mPE", _mpe)
            v.tensor_reduce(srow[0:1, 0:1], psc[0:1, 0:2048], AXX, ALU.max)
            v.tensor_reduce(srow[0:1, 1:2], psc2[0:1, 0:2048], AXX, ALU.max)
            v.tensor_tensor(srow[0:1, 0:1], srow[0:1, 0:1], srow[0:1, 1:2],
                            ALU.max)
            ins = v.tensor_tensor(srow[0:1, 0:1], srow[0:1, 0:1],
                                  srow[0:1, 3:4], ALU.add)
            P.inc(ins, "mDVE")
        mdve_c = P.cnt["mDVE"]

        @block.tensor
        def _(t, _mdve=mdve_c):
            P.w(t, "mDVE", _mdve)
            ins = t.matmul(psc2[:, 1024:1025], negc_row[:], srow[0:1, 0:1],
                           start=True, stop=True)
            P.inc(ins, "mPE")
        mpe_c = P.cnt["mPE"]

        exp_prev = P.cnt["exp"]

        @block.vector
        def _(v, _mpe=mpe_c, _exp=exp_prev):
            P.w(v, "mPE", _mpe)
            P.w(v, "exp", _exp)
            ins = v.tensor_copy(bias_hh[:], psc2[:, 1024:1025])
            P.inc(ins, "bia")
        bia_now = P.cnt["bia"]

        for h in range(HC):
            for qh in range(2):
                n = (qb * 2 + qh + 1) * 4
                npair = n // 2
                qkt0 = P.cnt["qkt"]
                exp0 = P.cnt["exp"]
                av0 = P.cnt["av"]
                msk0 = P.cnt["msk"]
                mdve_fin = P.cnt["mDVE"]
                qs = h * 1024 + qh * 512

                def vtile(i, _h=h):
                    return v4[:, _h * S + i * 128: _h * S + i * 128 + 128]

                def ktile(i, _h=h):
                    return K4[:, _h * S + i * 128: _h * S + i * 128 + 128]

                def consume(t, j, _n=n, _exp0=exp0, _msk0=msk0):
                    P.w(t, "exp", _exp0 + j + 1)
                    ndiag = max(0, 2 * j + 2 - (_n - 4))
                    if ndiag > 0:
                        P.w(t, "msk", _msk0 + ndiag)
                    for tt in (2 * j, 2 * j + 1):
                        sl = probt[j % 2][:, (tt % 2) * 512:(tt % 2) * 512 + 512]
                        st, sp = (tt == 0), (tt == _n - 1)
                        t.matmul(ps_s, ones_col_b[:], sl, start=st, stop=sp)
                        ins = t.matmul(pav, vtile(tt), sl, start=st, stop=sp)
                        P.inc(ins, "av")

                @block.tensor
                def _(t, _n=n, _qs=qs, _exp0=exp0, _mdve=mdve_fin,
                      _bia=bia_now, consume=consume, ktile=ktile):
                    P.w(t, "mDVE", _mdve)
                    P.w(t, "bia", _bia)
                    for i in range(_n):
                        if i >= 4:
                            P.w(t, "exp", _exp0 + (i - 4) // 2 + 1)
                        ins = t.matmul(psc[:, (i % 4) * 512:(i % 4) * 512 + 512],
                                       ktile(i), qbuf4[:, _qs:_qs + 512],
                                       start=True, stop=True)
                        P.inc(ins, "qkt")
                        if i % 2 == 1 and i >= 3:
                            consume(t, (i - 1) // 2 - 1)
                    consume(t, _n // 2 - 1)

                @block.scalar
                def _(sc, _npair=npair, _qkt0=qkt0, _av0=av0, _bia=bia_now):
                    P.w(sc, "bia", _bia)
                    for j in range(_npair):
                        P.w(sc, "qkt", _qkt0 + 2 * j + 2)
                        if j >= 2:
                            P.w(sc, "av", _av0 + 2 * (j - 2) + 2)
                        ins = sc.activation(
                            probt[j % 2][:],
                            psc[:, (j % 2) * 1024:(j % 2) * 1024 + 1024],
                            AF.Exp, bias=bias_hh[:, 0:1], scale=SCALE)
                        P.inc(ins, "exp")

                @block.vector
                def _(v, _n=n, _exp0=exp0):
                    P.w(v, "mPOOL", 1)
                    for d in range(4):
                        i = _n - 4 + d
                        j = i // 2
                        P.w(v, "exp", _exp0 + j + 1)
                        sl = probt[j % 2][:, (i % 2) * 512:(i % 2) * 512 + 512]
                        ins = v.tensor_tensor(sl, sl,
                                              mask4[:, d * 512:(d + 1) * 512],
                                              ALU.mult)
                        P.inc(ins, "msk")

                av_all = P.cnt["av"]
                gidx = (P.cnt["saA"] + P.cnt["saB"]) // 16

                @block.vector
                def _(v, _av=av_all):
                    P.w(v, "av", _av)
                    ins = v.reciprocal(rec_row[:], ps_s)
                    P.inc(ins, "mDVE")
                mdve3 = P.cnt["mDVE"]

                @block.tensor
                def _(t, _mdve=mdve3):
                    P.w(t, "mDVE", _mdve)
                    ins = t.matmul(psc[:, 0:512], ones_row_f[:], rec_row[:],
                                   start=True, stop=True)
                    P.inc(ins, "mPE")
                mpe4 = P.cnt["mPE"]

                abuf_t = attn_b[gidx % 2]
                sas = "saA" if gidx % 2 == 0 else "saB"

                @block.vector
                def _(v, _abuf=abuf_t, _g=gidx, _sas=sas, _mpe=mpe4):
                    P.w(v, "mPE", _mpe)
                    v.tensor_copy(recB[:], psc[:, 0:512])
                    if _g >= 2:
                        P.w(v, _sas, 16 * (_g // 2))
                    ins = v.tensor_tensor(_abuf[:], pav, recB[:], ALU.mult)
                    P.inc(ins, "mDVE")
                mdve4 = P.cnt["mDVE"]

                gcol = b * S + (qb * 2 + qh) * 512
                j2 = gcol // 1024
                coff = gcol % 1024

                @block.scalar
                def _(sc, _abuf=abuf_t, _h=h, _j2=j2, _coff=coff, _mdve=mdve4,
                      _sas=sas):
                    P.w(sc, "mDVE", _mdve)
                    P.inc(sc.dma_start(
                        out=attn_a2a.ap()[_j2 * 512 + _h * 128:
                                          _j2 * 512 + (_h + 1) * 128,
                                          _coff:_coff + 512],
                        in_=_abuf[:]), _sas, 16)

            for _ in range(2):
                if bg:
                    bg.pop(0)()


def make_dequant_chunks(nc, block, P, pw, scale_sb, pkq, shq, dqq, w_scr):
    chunks = []

    def chunk(och):
        dq_before = och

        @block.gpsimd
        def _(g, _och=och):
            if _och > 0:
                P.w(g, "dq", _och)
            P.inc(g.dma_start(out=pkq[:],
                              in_=pw[_och * 128:(_och + 1) * 128, :]), "dG", 16)
        dg_load = P.cnt["dG"]

        @block.vector
        def _(v, _och=och, _dg=dg_load):
            if _och == 0:
                P.w(v, "st")
            P.w(v, "dG", _dg)
            scv = scale_sb[:, _och * 32:(_och + 1) * 32].unsqueeze(2) \
                .broadcast_to([128, 32, 16])
            for e in range(8):
                v.tensor_single_scalar(shq[:], pkq[:], 4 * e,
                                       ALU.logical_shift_left)
                v.tensor_single_scalar(shq[:], shq[:], 28,
                                       ALU.arith_shift_right)
                out_v = dqq[:].rearrange("p (g s e) -> p g s e",
                                         g=32, s=16, e=8)[:, :, :, e]
                in_v = shq[:].rearrange("p (g s) -> p g s", s=16)
                if DEQ_CONTIG_TEST:
                    out_v = dqq[:, e * 512:(e + 1) * 512].rearrange(
                        "p (g s) -> p g s", s=16)
                ins = v.tensor_tensor(out_v, in_v, scv, ALU.mult)
            P.inc(ins, "dq")

        @block.gpsimd
        def _(g, _och=och):
            P.w(g, "dq", _och + 1)
            P.inc(g.dma_start(out=w_scr.ap()[_och * 128:(_och + 1) * 128, :],
                              in_=dqq[:]), "dG", 16)

    for och in range(NCH):
        chunks.append(lambda _o=och: chunk(_o))
    return chunks


def emit_w_transpose(nc, block, P, w_scr, wt_scr, wttq):
    dG_all = P.cnt["dG"]

    @block.sync
    def _(sy):
        P.w(sy, "dG", dG_all)
        for ich in range(NCH):
            P.inc(sy.dma_start(out=wttq[:],
                               in_=w_scr.ap()[:, ich * 128:(ich + 1) * 128],
                               transpose=True), "dS", 16)
            P.w(sy, "dS")
            P.inc(sy.dma_start(out=wt_scr.ap()[ich * 128:(ich + 1) * 128, :],
                               in_=wttq[:]), "dS", 16)
            P.w(sy, "dS")


def emit_phase_E(nc, block, P, attn_own, wo_in, x_fm, x1_scr, abuf,
                 wldE, wcb, xch, x1c, pwoA, pwoB, x1snap=None, absnap=None,
                 recB=None):
    cc_all = P.cnt["cc"]

    @block.sync
    def _(sy):
        P.w(sy, "cc", cc_all)
        P.w(sy, "ccd", P.cnt["ccd"])
        P.w(sy, "dG")
        P.w(sy, "dq")
        P.inc(sy.dma_start(
            out=abuf[:].rearrange("p (f t) -> p f t", t=1024),
            in_=attn_own.ap().rearrange("(f p) t -> p f t", p=128)), "ab", 16)
    ab_d = P.cnt["ab"]

    pw4 = [pwoA[:, 0:1024], pwoA[:, 1024:2048],
           pwoB[:, 0:1024], pwoB[:, 1024:2048]]

    if absnap is not None:
        @block.vector
        def _(v, _ab=ab_d):
            P.w(v, "ab", _ab)
            v.tensor_copy(absnap[:, 0:1024], abuf[:, 0:1024])
            ins = v.tensor_copy(absnap[:, 1024:2048], abuf[:, 16 * 1024:17 * 1024])
            P.inc(ins, "mDVE")
    else:
        @block.vector
        def _(v, _ab=ab_d):
            P.w(v, "ab", _ab)
            v.tensor_copy(recB[:], abuf[:, 0:512])
            ins = v.tensor_copy(recB[:], abuf[:, 16 * 1024:16 * 1024 + 512])
            P.inc(ins, "mDVE")
    snap_m = P.cnt["mDVE"]

    schedule = [1, 2, 3, 4, 5, 6, 7, 0, 0]
    for it, eg in enumerate(schedule):
        woc0 = P.cnt["woc"]
        woe0 = P.cnt["woe"]
        ew0 = (P.cnt["ewA"], P.cnt["ewB"])

        @block.sync
        def _(sy, _eg=eg, _woc0=woc0):
            for fch in range(NCH):
                if fch >= 2:
                    P.w(sy, "woc", _woc0 + fch - 1)
                P.inc(sy.dma_start(
                    out=wldE[fch % 2][:],
                    in_=wo_in[fch * 128:(fch + 1) * 128,
                              _eg * 512:(_eg + 1) * 512]),
                      "ewA" if fch % 2 == 0 else "ewB", 16)

        wof0 = P.cnt["wof"]

        @block.vector
        def _(v, _ew0=ew0, _it=it):
            for fch in range(NCH):
                g = _it * NCH + fch
                P.w(v, "ewA" if fch % 2 == 0 else "ewB",
                    _ew0[fch % 2] + 16 * (fch // 2 + 1))
                if g >= 2:
                    P.w(v, "wof", g - 1)
                ins = v.tensor_copy(wcb[fch % 2][:], wldE[fch % 2][:])
                P.inc(ins, "woc")
        woc_b = P.cnt["woc"] - NCH

        @block.tensor
        def _(t, _woc=woc_b, _ab=ab_d, _woe=woe0, _sm=snap_m):
            P.w(t, "ab", _ab)
            if _sm is not None:
                P.w(t, "mDVE", _sm)
            P.w(t, "woe", _woe)
            ins = None
            for fch in range(NCH):
                P.w(t, "woc", _woc + fch + 1)
                st, sp = (fch == 0), (fch == NCH - 1)
                for e4 in range(4):
                    lh = wcb[fch % 2][:, e4 * 128:(e4 + 1) * 128]
                    t.matmul(pw4[e4][:, 0:512], lh,
                             abuf[:, fch * 1024:fch * 1024 + 512],
                             start=st, stop=sp)
                    ins = t.matmul(pw4[e4][:, 512:1024], lh,
                                   abuf[:, fch * 1024 + 512:fch * 1024 + 1024],
                                   start=st, stop=sp)
                P.inc(ins, "wof")
        wof_e = P.cnt["wof"]

        ex0 = (P.cnt["exA"], P.cnt["exB"])

        @block.sync
        def _(sy, _eg=eg, _it=it):
            for e4 in range(4):
                kg = _it * 4 + e4
                k = _eg * 4 + e4
                if kg >= 2:
                    P.w(sy, "woe", kg - 1)
                P.inc(sy.dma_start(
                    out=xch[e4 % 2][:],
                    in_=x_fm[k * 128:(k + 1) * 128, :]),
                      "exA" if e4 % 2 == 0 else "exB", 16)

        @block.vector
        def _(v, _it=it, _mpe=wof_e, _ex0=ex0):
            P.w(v, "wof", _mpe)
            for e4 in range(4):
                kg = _it * 4 + e4
                P.w(v, "exA" if e4 % 2 == 0 else "exB",
                    _ex0[e4 % 2] + 16 * (e4 // 2 + 1))
                if kg >= 2:
                    P.w(v, "sxA" if kg % 2 == 0 else "sxB", 16 * (kg // 2))
                ins = v.tensor_add(x1c[e4 % 2][:], pw4[e4], xch[e4 % 2][:])
                P.inc(ins, "woe")
        woe_n = P.cnt["woe"]

        @block.scalar
        def _(sc, _eg=eg, _it=it, _woe=woe_n):
            for e4 in range(4):
                kg = _it * 4 + e4
                k = _eg * 4 + e4
                P.w(sc, "woe", _woe - 3 + e4)
                P.inc(sc.dma_start(
                    out=x1_scr.ap()[k * 128:(k + 1) * 128, :],
                    in_=x1c[e4 % 2][:]),
                      "sxA" if kg % 2 == 0 else "sxB", 16)
            if _eg == 0 and x1snap is not None:
                P.w(sc, "sxA")
                P.w(sc, "sxB")
                P.inc(sc.dma_start(out=x1snap[:], in_=x1_scr.ap()[0:128, :]),
                      "dA", 16)


def emit_phase_F(nc, block, P, wt_scr, x1_scr, lnt2, bias_sb, out_fm,
                 wts, x1n, outb, pfA, pfB):
    pf4 = [pfA[:, 0:1024], pfA[:, 1024:2048],
           pfB[:, 0:1024], pfB[:, 1024:2048]]
    nrm_all = P.cnt["nrm"]

    for og in range(8):
        ff0 = P.cnt["ff"]
        ffe0 = P.cnt["ffe"]
        fw0 = (P.cnt["fwA"], P.cnt["fwB"])

        @block.sync
        def _(sy, _og=og, _ff0=ff0):
            for ich in range(NCH):
                if ich >= 2:
                    P.w(sy, "ff", _ff0 + ich - 1)
                P.inc(sy.dma_start(
                    out=wts[ich % 2][:],
                    in_=wt_scr.ap()[ich * 128:(ich + 1) * 128,
                                    _og * 512:(_og + 1) * 512]),
                      "fwA" if ich % 2 == 0 else "fwB", 16)

        @block.tensor
        def _(t, _fw0=fw0, _ffe=ffe0):
            P.w(t, "nrm", nrm_all)
            P.w(t, "ffe", _ffe)
            ins = None
            for ich in range(NCH):
                P.w(t, "fwA" if ich % 2 == 0 else "fwB",
                    _fw0[ich % 2] + 16 * (ich // 2 + 1))
                st, sp = (ich == 0), (ich == NCH - 1)
                for o4 in range(4):
                    lh = wts[ich % 2][:, o4 * 128:(o4 + 1) * 128]
                    t.matmul(pf4[o4][:, 0:512], lh,
                             lnt2[:, ich * 1024:ich * 1024 + 512],
                             start=st, stop=sp)
                    ins = t.matmul(pf4[o4][:, 512:1024], lh,
                                   lnt2[:, ich * 1024 + 512:ich * 1024 + 1024],
                                   start=st, stop=sp)
                P.inc(ins, "ff")
        ff_done = P.cnt["ff"]

        fx0 = (P.cnt["fxA"], P.cnt["fxB"])

        @block.sync
        def _(sy, _og=og, _ffe0=ffe0):
            for o4 in range(4):
                k = _og * 4 + o4
                if k >= 2:
                    P.w(sy, "ffe", k - 1)
                P.inc(sy.dma_start(
                    out=x1n[o4 % 2][:],
                    in_=x1_scr.ap()[k * 128:(k + 1) * 128, :]),
                      "fxA" if o4 % 2 == 0 else "fxB", 16)

        @block.vector
        def _(v, _og=og, _ff=ff_done, _fx0=fx0):
            P.w(v, "ff", _ff)
            if _og == 0:
                P.w(v, "st")
            for o4 in range(4):
                k = _og * 4 + o4
                och = k
                P.w(v, "fxA" if o4 % 2 == 0 else "fxB",
                    _fx0[o4 % 2] + 16 * (o4 // 2 + 1))
                if k >= 2:
                    P.w(v, "soA" if k % 2 == 0 else "soB", 16 * (k // 2))
                ins = v.scalar_tensor_tensor(outb[o4 % 2][:], pf4[o4],
                                             bias_sb[:, och:och + 1],
                                             x1n[o4 % 2][:], ALU.add, ALU.add)
                P.inc(ins, "ffe")
        ffe_n = P.cnt["ffe"]

        @block.scalar
        def _(sc, _og=og, _ffe=ffe_n):
            for o4 in range(4):
                k = _og * 4 + o4
                P.w(sc, "ffe", _ffe - 3 + o4)
                P.inc(sc.dma_start(
                    out=out_fm[k * 128:(k + 1) * 128, :],
                    in_=outb[o4 % 2][:]),
                      "soA" if k % 2 == 0 else "soB", 16)


_DBG_N = [0]


def emit_dbg(nc, block, P, dbg, dram_specs=None, sbuf_specs=None, wait=None,
             sbuf_f32=None, wait_mdve=False, sbuf_extra=None):
    _DBG_N[0] += 1
    with nc.sbuf_tensor(f"dbgf{_DBG_N[0]}", [128, 8192], F32) as dbgf:
        @block.vector
        def _(v):
            P.inc(v.memset(dbgf[:], 0.0), "mDVE")
        md0 = P.cnt["mDVE"]

        if dram_specs:
            with nc.sbuf_tensor(f"dbgb{_DBG_N[0]}", [128, 8192], BF16) as dbgb:
                @block.sync
                def _(sy):
                    P.w(sy, "mDVE", md0)
                    if wait:
                        P.w(sy, wait)
                    for i, (src, row, dt) in enumerate(dram_specs):
                        if dt == BF16:
                            P.inc(sy.dma_start(
                                out=dbgb[:, i * 1024:(i + 1) * 1024],
                                in_=src[row:row + 128, 0:1024]), "dS", 16)
                        else:
                            P.inc(sy.dma_start(
                                out=dbgf[:, i * 1024:(i + 1) * 1024],
                                in_=src[row:row + 128, 0:1024]), "dS", 16)

                @block.vector
                def _(v):
                    P.w(v, "dS")
                    ins = None
                    for i, (src, row, dt) in enumerate(dram_specs):
                        if dt == BF16:
                            ins = v.tensor_copy(dbgf[:, i * 1024:(i + 1) * 1024],
                                                dbgb[:, i * 1024:(i + 1) * 1024])
                    if sbuf_extra:
                        n0 = len(dram_specs)
                        for i, (srcb, off) in enumerate(sbuf_extra):
                            ins = v.tensor_copy(
                                dbgf[:, (n0 + i) * 1024:(n0 + i + 1) * 1024],
                                srcb[:, off:off + 1024])
                    if ins is None:
                        ins = v.engine_nop()
                    P.inc(ins, "mDVE")
        else:
            @block.vector
            def _(v):
                if wait_mdve:
                    P.w(v, "mDVE")
                ins = None
                for i, (src, off) in enumerate(sbuf_specs):
                    w = 1024
                    if src.shape[0] < 128 or src.shape[1] < off + 1024:
                        w = src.shape[1] - off
                    ins = v.tensor_copy(dbgf[:src.shape[0], i * 1024:i * 1024 + w],
                                        src[:src.shape[0], off:off + w])
                P.inc(ins, "mDVE")

        @block.sync
        def _(sy):
            P.w(sy, "mDVE")
            P.inc(sy.dma_start(out=dbg, in_=dbgf[:]), "dS", 16)


def emit_finish(nc, block, P, out_fm):
    with nc.sbuf_tensor("zout", [128, 1024], F32) as z:
        @block.vector
        def _(v):
            P.w(v, "dS")
            P.w(v, "dA")
            P.inc(v.memset(z[:], 7.0), "mDVE")

        @block.sync
        def _(sy):
            P.w(sy, "mDVE")
            for ch in range(NCH):
                P.inc(sy.dma_start(out=out_fm[ch * 128:(ch + 1) * 128, :],
                                   in_=z[:]), "dS", 16)
            P.w(sy, "dS")
            P.w(sy, "dA")
            P.w(sy, "dG")


def kernel(L0_attn_hidden_states, L0_attn_wq, L0_attn_wkv, L0_attn_wo,
           L0_ffn_packed_weight, L0_ffn_scale, L0_ffn_bias):
    x = np.asarray(L0_attn_hidden_states, dtype=np.float32)
    wq = np.asarray(L0_attn_wq, dtype=np.float32)
    wkv = np.asarray(L0_attn_wkv, dtype=np.float32)
    wo_h = np.ascontiguousarray(np.asarray(L0_attn_wo, dtype=np.float32))
    pw_h = np.ascontiguousarray(np.asarray(L0_ffn_packed_weight, dtype=np.int32))
    sc_h = np.ascontiguousarray(np.asarray(L0_ffn_scale, dtype=np.float32))
    fb_h = np.ascontiguousarray(np.asarray(L0_ffn_bias, dtype=np.float32))
    ident = np.eye(128, dtype=ml_dtypes.bfloat16)

    xf = x.reshape(B * S, D)
    in_maps = []
    for c in range(NCORES):
        heads = range(HC * c, HC * c + HC)
        qcols = np.concatenate([np.arange(h * 192, h * 192 + 128) for h in heads])
        kcols = np.concatenate([np.arange(h * 128, h * 128 + 128) for h in heads])
        in_maps.append({
            "x_fm": np.ascontiguousarray(xf[c * TOWN:(c + 1) * TOWN].T),
            "wq_s": np.ascontiguousarray(wq[:, qcols]),
            "wk_s": np.ascontiguousarray(wkv[:, kcols]),
            "wv_s": np.ascontiguousarray(wkv[:, 32 * 128 + kcols]),
            "wo": wo_h, "pw": pw_h, "qsc": sc_h, "fbias": fb_h, "ident": ident,
        })

    nc = build_program(DEBUG_PHASE)
    res = run_bass_kernel_spmd(nc, in_maps, core_ids=list(range(NCORES)))
    out = np.empty((B * S, D), dtype=np.float32)
    for c in range(NCORES):
        out[c * TOWN:(c + 1) * TOWN] = res.results[c]["out_fm"].T
    if DEBUG_PHASE:
        kernel.dbg = [r.get("dbg") for r in res.results]
    return out.reshape(B, S, D)
